# version 22
# baseline (speedup 1.0000x reference)
import math
import contextlib
import numpy as np
import ml_dtypes
import concourse.bass as bass
import concourse.mybir as mybir
from concourse.bass_utils import run_bass_kernel_spmd

F32 = mybir.dt.float32
BF16 = mybir.dt.bfloat16
AF = mybir.ActivationFunctionType
ALU = mybir.AluOpType
AX = mybir.AxisListType.X

ERF_SCALE = float(np.float32(1.0 / math.sqrt(math.pi / 2.0)))
ERF_BIAS = float(np.float32(-math.sqrt(0.5) / math.sqrt(math.pi / 2.0)))
SQRT_DQK = float(np.float32(math.sqrt(128.0)))
MASKVAL = -30000.0


def _bucket_of(n):
    n = np.asarray(n)
    n_safe = np.maximum(n, 1).astype(np.float32)
    v = np.log(n_safe / np.float32(16.0)).astype(np.float32)
    v = (v / np.float32(math.log(8.0))).astype(np.float32)
    v = (v * np.float32(16.0)).astype(np.float32)
    vl = np.minimum(16 + v.astype(np.int32), 31)
    return np.where(n < 16, n, vl)


def _host_planes():
    p = np.arange(128)[:, None]
    f = np.arange(128)[None, :]
    d0 = f - p
    b0 = _bucket_of(np.maximum(d0, 0))
    tri = np.zeros((128, 33 * 128), np.float32)
    for b in range(32):
        tri[:, b * 128:(b + 1) * 128] = ((b0 == b) & (d0 >= 0)).astype(np.float32)
    tri[:, 32 * 128:] = (d0 < 0).astype(np.float32)
    d1 = 128 + f - p
    b1 = _bucket_of(d1)
    dd1 = np.zeros((128, 33 * 128), np.float32)
    for b in range(32):
        dd1[:, b * 128:(b + 1) * 128] = (b1 == b).astype(np.float32)
    return np.concatenate([tri, dd1], axis=1)


def _gsel(delta):
    if delta >= 2:
        return 2
    if delta == 1:
        return 1
    if delta == 0:
        return 0
    return 3


def _host_ohbnd(k):
    oh = np.zeros((16, 4), np.float32)
    for pos in range(8):
        oh[pos, _gsel(k + 4 - pos)] = 1.0
        oh[8 + pos, _gsel(7 - k - pos)] = 1.0
    return np.repeat(oh.reshape(1, 64), 128, axis=0).astype(np.float32)


def _own_tiles(k, NT):
    M = NT // 8
    tiles = [NT - 1 - 4 * m - k for m in range(M)] + [4 * m + k for m in range(M)]
    exts = [NT - 4 * m for m in range(M)] + [4 * m + 4 for m in range(M)]
    return tiles, exts


def build_nc(NT, D, FF, trunc=None):
    S = NT * 128
    M = NT // 8
    OWN = 2 * M
    DC = D // 128
    FC = FF // 128
    NCT = NT + OWN
    NG = NCT // 4
    NGC = NT // 4
    HBASE = NT // 2
    INV_S = float(np.float32(1.0 / S))
    NPR = 5 + FC + DC
    W1CH = FF // 128
    W2CQ = FF // 1024
    W2CH = DC * W2CQ
    HD = D // 512
    _, exts0 = _own_tiles(0, NT)

    nc = bass.Bass("TRN2", target_bir_lowering=False)
    ap = {}

    def din(name, shape, dt=F32):
        ap[name] = nc.dram_tensor(name, shape, dt, kind="ExternalInput").ap()

    din("xcov", [NCT * 128, D])
    din("wqk", [D, 128])
    din("wv", [D, D])
    din("bv", [D])
    din("relb", [32])
    din("W1", [FF, D])
    din("W2", [D, FF])
    din("b2v", [D])
    din("ppack", [NPR, 128])
    din("planes", [128, 2 * 33 * 128])
    din("ohbnd", [128, 64])
    din("eyef", [128, 128])
    din("eyeb", [128, 128], BF16)
    out_hbm = nc.dram_tensor("out", [OWN * 128, D], F32, kind="ExternalOutput").ap()

    xn1_d = nc.dram_tensor("xn1_d", [NCT * 128, D], BF16, kind="Internal").ap()
    vhi_d = nc.dram_tensor("vhi_d", [128, (NT - HBASE) * D], BF16, kind="Internal").ap()
    att_d = nc.dram_tensor("att_d", [OWN * 128, D], BF16, kind="Internal").ap()
    x2_d = nc.dram_tensor("x2_d", [OWN * 128, D], F32, kind="Internal").ap()
    xn3_d = nc.dram_tensor("xn3_d", [OWN * 128, D], BF16, kind="Internal").ap()
    y_d = nc.dram_tensor("y_d", [OWN * 128, D], BF16, kind="Internal").ap()
    w1t_d = nc.dram_tensor("w1t_d", [FF, D], BF16, kind="Internal").ap()
    w2t_d = nc.dram_tensor("w2t_d", [D, FF], BF16, kind="Internal").ap()

    VWC = max(HBASE * D, DC * FF)
    WVHC = max(DC * D, 8 * D)

    ctx = contextlib.ExitStack()
    with ctx:
        sbt = lambda name, shape, dt, **kw: ctx.enter_context(
            nc.sbuf_tensor(name, shape, dt, **kw))

        vw = sbt("vw", [128, VWC], BF16, is_transpose_dest=True)
        wvh = sbt("wvh", [128, WVHC], BF16)
        kT = sbt("kT", [128, NT * 128], BF16)
        qT = sbt("qT", [128, OWN * 128], BF16)
        XNTC = max(2 * DC * 512, 6144)
        xnT = sbt("xnT", [128, XNTC], BF16, is_transpose_dest=True)
        war = sbt("war", [128, 16384], BF16, is_transpose_dest=True)
        x_in = sbt("x_in", [128, 2 * D], F32)
        xr_buf = sbt("xr_buf", [128, 2 * D], F32)
        evac = sbt("evac", [128, 2 * D], F32)
        sq_scr = sbt("sq_scr", [128, max(D, 1024)], F32)
        bnd = sbt("bnd", [128, 2048], F32)
        farw = sbt("farw", [128, 512], F32)
        s1x = sbt("s1x", [128, NCT], F32)
        s1q = sbt("s1q", [128, NCT], F32)
        r1 = sbt("r1", [128, NCT], F32)
        n1 = sbt("n1", [128, NCT], F32)
        st2 = sbt("st2", [128, 4 * OWN + 2], F32)
        st3 = sbt("st3", [128, 4 * OWN], F32)
        st4 = sbt("st4", [128, 4 * OWN + 4], F32)
        onesc = sbt("onesc", [128, 512], F32)
        vstg = sbt("vstg", [128, 2 * D], BF16)
        w2w = sbt("w2w", [128, 2 * D], BF16, is_transpose_dest=True)
        pstage = sbt("pstage", [128, 128], F32)
        eye_t = sbt("eye_t", [128, 128], F32)
        eye_b = sbt("eye_b", [128, 128], BF16)
        pcols = sbt("pcols", [128, NPR], F32)
        pc2 = sbt("pc2", [128, 12], F32)
        wabs = sbt("wabs", [128, 64], F32)
        wred = sbt("wred", [128, 2], F32)
        cscal = sbt("cscal", [1, 2], F32)
        relrow = sbt("relrow", [1, 33], F32)
        bvrow = sbt("bvrow", [1, D], BF16)
        b2r = sbt("b2r", [1, D], BF16)
        ones_bfr = sbt("ones_bfr", [1, 128], BF16)
        ones_fc = sbt("ones_fc", [128, 1], F32)
        plane_t = sbt("plane_t", [128, 6 * 128], F32)

        warf = war.bitcast(F32)
        w_f = [warf[:, 0:1024], warf[:, 1024:2048]]
        w_le = [warf[:, 2048:3072], warf[:, 3072:4096]]
        w_tb = [war[:, 8192:9216], war[:, 9216:10240]]
        qks = war[:, 10240:11264]
        wqk_b = war[:, 11264:11264 + DC * 128]
        masters = warf[:, 6144:6656]
        t33 = warf[:, 6656:6689]
        w_f4 = [warf[:, 0:1024], warf[:, 1024:2048],
                warf[:, 6144:7168], warf[:, 7168:8192]]
        hT = war[:, 0:FC * 512]
        wv_b = wvh[:, 0:DC * D]
        xnTf = xnT.bitcast(F32)
        simb = [xnTf[:, 0:512], xnTf[:, 512:1024]]
        attn_b = [xnT[:, 2048:2560], xnT[:, 2560:3072]]
        attn2 = [xnT[:, 3072:3584], xnT[:, 3584:4096]]
        xn_b = xr_buf.bitcast(BF16)
        x3bf = x_in.bitcast(BF16)
        evbf = evac.bitcast(BF16)

        ps_o0 = ctx.enter_context(nc.psum_tensor("ps_o0", [128, 1024], F32))
        ps_o1 = ctx.enter_context(nc.psum_tensor("ps_o1", [128, 1024], F32))
        ps_a = ctx.enter_context(nc.psum_tensor("ps_a", [128, 512], F32))
        ps_b = ctx.enter_context(nc.psum_tensor("ps_b", [128, 512], F32))
        ps_c = ctx.enter_context(nc.psum_tensor("ps_c", [128, 512], F32))
        ps_d = ctx.enter_context(nc.psum_tensor("ps_d", [128, 512], F32))
        ps_o = [ps_o0, ps_o1]
        ps_ab = [ps_a, ps_b]
        ps_cd = [ps_c, ps_d]

        SEMS = {k: ctx.enter_context(nc.semaphore(k)) for k in ("S", "V", "T", "G")}
        STRC = {}

        def _stream_sem(tag, par):
            key = f"{tag}{par}"
            if key not in SEMS:
                SEMS[key] = ctx.enter_context(nc.semaphore(key))
            return key

        SCHED = []

        def op(eng, fn, waits=(), sig=None, sem="?", inc=1, selfwait=None):
            SCHED.append((eng, fn, tuple(waits), sig, sem, inc, selfwait))

        def t_op(fn, waits=(), sig=None):
            op("tensor", fn, waits, sig, "T")

        def s_op(fn, waits=(), sig=None):
            op("scalar", fn, waits, sig, "S")

        def v_op(fn, waits=(), sig=None):
            op("vector", fn, waits, sig, "V")

        def g_op(fn, waits=(), sig=None):
            op("gpsimd", fn, waits, sig, "G")

        NPAR = {"LX": 4, "LN": 4, "LW": 4, "LV": 4, "T0": 4, "TW": 4, "S0": 4, "LA": 4}

        def _dma(eng, stream, fn, waits, sig):
            i = STRC.get(stream, 0)
            STRC[stream] = i + 1
            np_ = NPAR.get(stream, 2)
            semkey = _stream_sem(stream, i % np_)
            val = 16 * (i // np_ + 1)
            selfw = (semkey, val - 16) if val > 16 else None
            op(eng, fn, waits, sig, semkey, 16, selfw)

        def ld_dma(fn, waits=(), sig=None, stream="L0"):
            _dma("sync", stream, fn, waits, sig)

        def td_dma(fn, waits=(), sig=None, stream="T0"):
            _dma("scalar", stream, fn, waits, sig)

        def st_dma(fn, waits=(), sig=None, stream="S0"):
            _dma("gpsimd", stream, fn, waits, sig)

        def gd_dma(fn, waits=(), sig=None, stream="LN"):
            _dma("gpsimd", stream, fn, waits, sig)

        v_op(lambda e: e.memset(onesc[:], 1.0), (), "ones")
        v_op(lambda e: e.memset(ones_fc[:], 1.0), (), "onesfc")
        v_op(lambda e: e.memset(ones_bfr[:], 1.0), (), "onesbf")
        v_op(lambda e: e.memset(pstage[:], 0.0), (), "pstz")
        v_op(lambda e: e.memset(pc2[:, 4:5], 1e-5), (), "eps")
        v_op(lambda e: e.memset(pc2[:, 5:6], ERF_SCALE), (), "erfS")
        v_op(lambda e: e.memset(pc2[:, 6:7], ERF_BIAS), (), "erfB")
        v_op(lambda e: e.memset(pc2[:, 11:12], 1.0), (), "one1")
        ld_dma(lambda e: e.dma_start(out=eye_t[:], in_=ap["eyef"]), (), "eye")
        ld_dma(lambda e: e.dma_start(out=eye_b[:], in_=ap["eyeb"]), (), "eyb")
        ld_dma(lambda e: e.dma_start(out=pstage[0:NPR, :], in_=ap["ppack"]), ("pstz",), "ppk")
        ld_dma(lambda e: e.dma_start(
            out=relrow[:, 0:32], in_=ap["relb"].rearrange("(o t) -> o t", o=1)), (), "rel")
        ld_dma(lambda e: e.dma_start(out=farw[:, 0:64], in_=ap["ohbnd"]), (), "ohb")
        for dc in range(DC):
            ld_dma(lambda e, dc=dc: e.dma_start(
                out=w_f[dc % 2][:, 0:128],
                in_=ap["wqk"][dc * 128:(dc + 1) * 128, :]),
                (f"cwqk{dc - 2}",) if dc >= 2 else (), f"lwqk{dc}")
            v_op(lambda e, dc=dc: e.tensor_copy(
                wqk_b[:, dc * 128:(dc + 1) * 128], w_f[dc % 2][:, 0:128]),
                (f"lwqk{dc}",), f"cwqk{dc}")
        for dc in range(DC):
            ld_dma(lambda e, dc=dc: e.dma_start(
                out=w_f[dc % 2][:, 0:D], in_=ap["wv"][dc * 128:(dc + 1) * 128, :]),
                (f"cwv{dc - 2}",) if dc >= 2 else (f"cwqk{DC - 1}",), f"lwv{dc}")
            v_op(lambda e, dc=dc: e.tensor_copy(
                wv_b[:, dc * D:(dc + 1) * D], w_f[dc % 2][:, 0:D]),
                (f"lwv{dc}",), f"cwv{dc}")
        st_dma(lambda e: e.dma_start(
            out=bvrow[:], in_=ap["bv"].rearrange("(o t) -> o t", o=1)), (), "bvr",
            stream="SB")
        st_dma(lambda e: e.dma_start(
            out=b2r[:], in_=ap["b2v"].rearrange("(o t) -> o t", o=1)), (), "b2r",
            stream="SB")
        v_op(lambda e: e.memset(relrow[:, 32:33], 1.0), ("rel",), "rel1")
        t_op(lambda e: e.transpose(ps_c[0:128, 0:128], pstage[:], eye_t[:]),
             ("ppk", "eye"), "ptr")
        v_op(lambda e: e.tensor_copy(pcols[:], ps_c[0:128, 0:NPR]), ("ptr",), "pcl")
        v_op(lambda e: e.tensor_scalar(pc2[:, 0:1], pcols[:, 1:2], INV_S, None, ALU.mult),
             ("pcl",), "gqS")
        v_op(lambda e: e.tensor_scalar(pc2[:, 1:2], pcols[:, 2:3], INV_S, None, ALU.mult),
             ("pcl",), "bqS")
        v_op(lambda e: e.tensor_copy(pc2[:, 2:3], pcols[:, 3:4]), ("pcl",), "gk")
        v_op(lambda e: e.tensor_copy(pc2[:, 3:4], pcols[:, 4:5]), ("pcl",), "bk")
        v_op(lambda e: e.tensor_scalar(relrow[:, 0:32], relrow[:, 0:32], SQRT_DQK, None,
                                       ALU.mult), ("rel",), "rsc")
        t_op(lambda e: e.matmul(ps_d[0:128, 0:33], onesc[0:1, 0:128], relrow[:],
                                start=True, stop=True), ("rsc", "rel1", "ones"), "t33m")
        v_op(lambda e: e.tensor_copy(t33[:], ps_d[0:128, 0:33]), ("t33m",), "t33c")

        if NCT >= 48:
            CBS = [0, 8, 28, 48, 68, NCT]
        else:
            CBS = [0, NCT]

        def chunk_of(t):
            for ci in range(len(CBS) - 1):
                if t < CBS[ci + 1]:
                    return ci
            return len(CBS) - 2

        mq = []
        for half, dst in ((0, masters[:, 0:128]), (1, masters[:, 128:256])):
            for b in range(33):
                pi = half * 33 + b
                pp = pi % 6
                wl = [f"mac{pi - 6}"] if pi >= 6 else ()

                def m_thunk(pi=pi, pp=pp, wl=wl, b=b, half=half, dst=dst):
                    ld_dma(lambda e: e.dma_start(
                        out=plane_t[:, pp * 128:(pp + 1) * 128],
                        in_=ap["planes"][:, pi * 128:(pi + 1) * 128]), wl, f"lpl{pi}")
                    if b == 0:
                        v_op(lambda e: e.tensor_scalar(
                            dst, plane_t[:, pp * 128:(pp + 1) * 128], t33[:, 0:1],
                            None, ALU.mult), (f"lpl{pi}", "t33c"), f"mac{pi}")
                    elif b == 32 and half == 0:
                        v_op(lambda e: e.scalar_tensor_tensor(
                            dst, plane_t[:, pp * 128:(pp + 1) * 128], MASKVAL, dst,
                            ALU.mult, ALU.add), (f"lpl{pi}",), f"mac{pi}")
                    else:
                        coef = t33[:, b:b + 1] if b < 32 else pc2[:, 11:12]
                        v_op(lambda e: e.scalar_tensor_tensor(
                            dst, plane_t[:, pp * 128:(pp + 1) * 128], coef, dst,
                            ALU.mult, ALU.add), (f"lpl{pi}",), f"mac{pi}")
                mq.append(m_thunk)

        def m_pre():
            v_op(lambda e: e.tensor_scalar(masters[:, 256:384], onesc[:, 0:128],
                                           t33[:, 31:32], None, ALU.mult),
                 ("t33c",), "mfar")
            v_op(lambda e: e.memset(masters[:, 384:512], MASKVAL), (), "mmask")
        mq.append(m_pre)
        for t in range(16):
            def b_thunk(t=t):
                for mi in range(4):
                    cc = t * 4 + mi
                    if mi == 0:
                        v_op(lambda e, t=t, cc=cc: e.tensor_scalar(
                            bnd[:, t * 128:(t + 1) * 128], masters[:, 0:128],
                            farw[:, cc:cc + 1], None, ALU.mult),
                            ("ohb", "mac65", "mfar", "mmask") if t == 0 else (), None)
                    else:
                        v_op(lambda e, t=t, mi=mi, cc=cc: e.scalar_tensor_tensor(
                            bnd[:, t * 128:(t + 1) * 128],
                            masters[:, mi * 128:(mi + 1) * 128],
                            farw[:, cc:cc + 1], bnd[:, t * 128:(t + 1) * 128],
                            ALU.mult, ALU.add), (), f"bnd{t}" if mi == 3 else None)
            mq.append(b_thunk)

        def m_farw():
            v_op(lambda e: e.tensor_scalar(farw[:], onesc[:], t33[:, 31:32], None,
                                           ALU.mult), ("bnd15",), "farw")
        mq.append(m_farw)
        MQN = len(mq)
        mq_i = [0]

        def mdrip(target):
            while mq_i[0] < min(target, MQN):
                mq[mq_i[0]]()
                mq_i[0] += 1

        def emit_stats(t, c):
            p2 = t % 2
            w = [f"sq1_{t - 2}", f"rd1_{t - 2}"] if t >= 2 else ()
            ld_dma(lambda e, t=t, p2=p2: e.dma_start(
                out=evac[:, p2 * D:(p2 + 1) * D],
                in_=ap["xcov"][t * 128:(t + 1) * 128, :]), w, f"ldx{t}", stream="LX")
            s_op(lambda e, t=t, p2=p2: e.activation(
                sq_scr[:], evac[:, p2 * D:(p2 + 1) * D], AF.Square,
                accum_out=s1q[:, t:t + 1]), (f"ldx{t}",), f"sq1_{t}")
            v_op(lambda e, t=t, p2=p2: e.tensor_reduce(
                s1x[:, t:t + 1], evac[:, p2 * D:(p2 + 1) * D], AX, ALU.add),
                (f"ldx{t}",), f"rd1_{t}")

        def emit_rstd_chunk(c):
            a, b = CBS[c], CBS[c + 1]
            wlast = (f"sq1_{b - 1}", f"rd1_{b - 1}")
            v_op(lambda e: e.tensor_scalar(n1[:, a:b], s1x[:, a:b], 1.0 / D, None,
                                           ALU.mult), wlast, f"m1c{c}")
            v_op(lambda e: e.tensor_scalar(s1q[:, a:b], s1q[:, a:b], 1.0 / D, None,
                                           ALU.mult), wlast, f"e21c{c}")
            v_op(lambda e: e.tensor_tensor(s1x[:, a:b], n1[:, a:b], n1[:, a:b],
                                           ALU.mult), (f"m1c{c}",), f"ms1c{c}")
            v_op(lambda e: e.tensor_tensor(s1q[:, a:b], s1q[:, a:b], s1x[:, a:b],
                                           ALU.subtract), (f"ms1c{c}", f"e21c{c}"),
                 f"var1c{c}")
            s_op(lambda e: e.activation(r1[:, a:b], s1q[:, a:b], AF.Sqrt,
                                        bias=pc2[:, 4:5], scale=pc2[:, 11:12]),
                 (f"var1c{c}", "eps", "one1"), f"sqrt1c{c}")
            v_op(lambda e: e.reciprocal(r1[:, a:b], r1[:, a:b]),
                 (f"sqrt1c{c}",), f"rc1c{c}")
            v_op(lambda e: e.scalar_tensor_tensor(n1[:, a:b], n1[:, a:b], -1.0,
                                                  r1[:, a:b], ALU.mult, ALU.mult),
                 (f"rc1c{c}",), f"nm1c{c}")

        def w1p1_chunk(ch):
            wp = ch % 2
            w = [f"w1a{ch - 2}"] if ch >= 2 else (f"cwv{DC - 1}",)
            gd_dma(lambda e, ch=ch, wp=wp: e.dma_start(
                out=w_f[wp][:, 0:D], in_=ap["W1"][ch * 128:(ch + 1) * 128, :]),
                w, f"w1l{ch}", stream="LW")
            s_op(lambda e, ch=ch, wp=wp: e.activation(
                sq_scr[:, 0:D], w_f[wp][:, 0:D], AF.Abs,
                accum_out=wabs[:, ch:ch + 1]), (f"w1l{ch}",), f"w1a{ch}")

        W1N = {"i": 0}

        def emit_norm(t, cidx):
            p2 = t % 2
            w = [f"nm1c{cidx}"]
            if t >= 2:
                w.append(f"nrm{t - 2}")
            gd_dma(lambda e, t=t, p2=p2: e.dma_start(
                out=x_in[:, p2 * D:(p2 + 1) * D],
                in_=ap["xcov"][t * 128:(t + 1) * 128, :]), w, f"ldn{t}", stream="LN")
            w2 = [f"ldn{t}"]
            if t >= 4:
                w2.append(f"txe{t - 4}")
            s_op(lambda e, t=t: e.activation(
                xn_b[:, (t % 4) * D:(t % 4 + 1) * D], x_in[:, p2 * D:(p2 + 1) * D],
                AF.Identity, bias=n1[:, t:t + 1], scale=r1[:, t:t + 1]), w2, f"nrm{t}")
            while W1N["i"] < W1CH and W1N["i"] < t + 1:
                w1p1_chunk(W1N["i"])
                W1N["i"] += 1
            if t >= 8:
                mdrip(4 * (t - 7))

        ps_tr = [ps_c.bitcast(BF16), ps_d.bitcast(BF16)]

        def emit_proj(gq):
            gp = gq % 2
            for tt in range(4):
                t = gq * 4 + tt
                tb = t % 2
                wtr = [f"nrm{t}", "eyb"]
                if t >= 2:
                    wtr.append(f"txe{t - 2}")
                for dc in range(DC):
                    t_op(lambda e, t=t, tb=tb, dc=dc: e.transpose(
                        ps_tr[tb][:, dc * 128:(dc + 1) * 128],
                        xn_b[:, (t % 4) * D + dc * 128:(t % 4) * D + (dc + 1) * 128],
                        eye_b[:]),
                        wtr if dc == 0 else (), f"txp{t}" if dc == DC - 1 else None)
                wev = [f"txp{t}"]
                if gq >= 2:
                    wev.append(f"pj{gq - 2}")
                for dc in range(DC):
                    v_op(lambda e, t=t, tb=tb, gp=gp, dc=dc, tt=tt: e.tensor_copy(
                        xnT[:, gp * DC * 512 + dc * 512 + tt * 128:
                            gp * DC * 512 + dc * 512 + tt * 128 + 128],
                        ps_tr[tb][:, dc * 128:(dc + 1) * 128]),
                        wev if dc == 0 else (), f"txe{t}" if dc == DC - 1 else None)
            t = gq * 4 + 3
            wtd = [f"txe{t}", f"txe{t - 1}", f"txe{t - 2}", f"txe{t - 3}"]
            if gq < NGC:
                for tt in range(4):
                    j = gq * 4 + tt
                    po = j % 2
                    wv_w = list(wtd) + [f"cwv{DC - 1}", "bvr", "onesbf"]
                    if j >= 2:
                        wv_w.append(f"vs{j - 2}")
                    for half in range(HD):
                        for dc in range(DC):
                            t_op(lambda e, po=po, gp=gp, dc=dc, half=half, tt=tt: e.matmul(
                                ps_o[po][:, half * 512:(half + 1) * 512],
                                xnT[:, gp * DC * 512 + dc * 512 + tt * 128:
                                    gp * DC * 512 + dc * 512 + tt * 128 + 128],
                                wv_b[:, dc * D + half * 512:dc * D + (half + 1) * 512],
                                start=(dc == 0), stop=False),
                                wv_w if (half == 0 and dc == 0) else ())
                        t_op(lambda e, po=po, half=half: e.matmul(
                            ps_o[po][:, half * 512:(half + 1) * 512], ones_bfr[:],
                            bvrow[0:1, half * 512:(half + 1) * 512],
                            start=False, stop=True),
                            (), f"tv{j}" if half == HD - 1 else None)
                    if j < HBASE:
                        s_op(lambda e, j=j, po=po: e.activation(
                            vw[:, j * D:(j + 1) * D], ps_o[po][:, 0:D], AF.Silu),
                            (f"tv{j}",), f"vs{j}")
                    else:
                        w4 = [f"tv{j}"]
                        if j >= HBASE + 2:
                            w4.append(f"svh{j - 2}")
                        s_op(lambda e, j=j, po=po: e.activation(
                            vstg[:, po * D:(po + 1) * D], ps_o[po][:, 0:D], AF.Silu),
                            w4, f"vs{j}")
                        td_dma(lambda e, j=j, po=po: e.dma_start(
                            out=vhi_d[:, (j - HBASE) * D:(j - HBASE + 1) * D],
                            in_=vstg[:, po * D:(po + 1) * D]), (f"vs{j}",),
                            f"svh{j}", stream="SV")
                wq = list(wtd) + [f"cwqk{DC - 1}", "pcl"]
                if gq >= 2:
                    wq.append(f"kaf{gq - 2}")
                for dc in range(DC):
                    t_op(lambda e, gq=gq, gp=gp, dc=dc: e.matmul(
                        ps_ab[gp][:, 0:512], wqk_b[:, dc * 128:(dc + 1) * 128],
                        xnT[:, gp * DC * 512 + dc * 512:gp * DC * 512 + (dc + 1) * 512],
                        start=(dc == 0), stop=(dc == DC - 1)),
                        wq if dc == 0 else (), f"tq{gq}" if dc == DC - 1 else None)
                s_op(lambda e, gq=gq, gp=gp: e.activation(
                    qks[:, gp * 512:(gp + 1) * 512], ps_ab[gp][:], AF.Silu,
                    bias=pcols[:, 0:1]), (f"tq{gq}",), f"qs{gq}")
                s_op(lambda e, gq=gq, gp=gp: e.activation(
                    kT[:, gq * 512:(gq + 1) * 512], qks[:, gp * 512:(gp + 1) * 512],
                    AF.Identity, bias=pc2[:, 3:4], scale=pc2[:, 2:3]),
                    (f"qs{gq}", "gk", "bk"), f"kaf{gq}")
            else:
                wq = list(wtd) + [f"cwqk{DC - 1}", "pcl"]
                if gq >= 2:
                    wq.append(f"qs{gq - 2}")
                for dc in range(DC):
                    t_op(lambda e, gq=gq, gp=gp, dc=dc: e.matmul(
                        ps_ab[gp][:, 0:512], wqk_b[:, dc * 128:(dc + 1) * 128],
                        xnT[:, gp * DC * 512 + dc * 512:gp * DC * 512 + (dc + 1) * 512],
                        start=(dc == 0), stop=(dc == DC - 1)),
                        wq if dc == 0 else (), f"tq{gq}" if dc == DC - 1 else None)
                s_op(lambda e, gq=gq, gp=gp: e.activation(
                    qks[:, gp * 512:(gp + 1) * 512], ps_ab[gp][:], AF.Silu,
                    bias=pcols[:, 0:1]), (f"tq{gq}",), f"qs{gq}")
                oq = gq - NGC
                s_op(lambda e, gq=gq, gp=gp, oq=oq: e.activation(
                    qT[:, oq * 512:(oq + 1) * 512], qks[:, gp * 512:(gp + 1) * 512],
                    AF.Identity, bias=pc2[:, 1:2], scale=pc2[:, 0:1]),
                    (f"qs{gq}", "gqS", "bqS"), f"qaf{gq}")
            op("tensor", lambda e: e.nop(), (f"tq{gq}",), f"pj{gq}", "T")

        NGALL = NCT // 4
        sc = CBS[1]
        for t in range(CBS[1]):
            emit_stats(t, 0)
        emit_rstd_chunk(0)
        rstd_done = 1
        ncur = [0]

        def flush_norms(limit):
            while ncur[0] < min(limit, NCT) and chunk_of(ncur[0]) < rstd_done:
                emit_norm(ncur[0], chunk_of(ncur[0]))
                ncur[0] += 1

        flush_norms(4)
        for gq in range(NGALL):
            target = min(NCT, (gq + 1) * 4 + (CBS[2] - CBS[1] if len(CBS) > 2 else NCT))
            while sc < target:
                emit_stats(sc, chunk_of(sc))
                sc += 1
                if rstd_done < len(CBS) - 1 and sc == CBS[rstd_done + 1]:
                    emit_rstd_chunk(rstd_done)
                    rstd_done += 1
            flush_norms((gq + 2) * 4)
            emit_proj(gq)
        mdrip(MQN)
        while W1N["i"] < W1CH:
            w1p1_chunk(W1N["i"])
            W1N["i"] += 1

        v_op(lambda e: e.tensor_reduce(wred[:, 0:1], wabs[:, 0:W1CH], AX, ALU.add),
             (f"w1a{W1CH - 1}",), "w1r")
        t_op(lambda e: e.matmul(ps_d[0:1, 0:1], ones_fc[:], wred[:, 0:1],
                                start=True, stop=True), ("w1r", "onesfc"), "w1t")
        v_op(lambda e: e.tensor_scalar(cscal[0:1, 0:1], ps_d[0:1, 0:1],
                                       0.5 / (FF * D), 5e-9, ALU.mult, ALU.max),
             ("w1t",), "w1c")
        t_op(lambda e: e.matmul(ps_d[0:128, 1:2], onesc[0:1, 0:128], cscal[0:1, 0:1],
                                start=True, stop=True), ("w1c",), "w1cb")
        v_op(lambda e: e.tensor_copy(pc2[:, 7:8], ps_d[0:128, 1:2]), ("w1cb",), "c1")
        v_op(lambda e: e.tensor_scalar(pc2[:, 8:9], pc2[:, 7:8], -1.0, None, ALU.mult),
             ("c1",), "nc1")

        def w2p1_chunk(ch):
            wp = ch % 4
            if ch >= 4:
                w = [f"w2a{ch - 4}"]
            elif ch < 2:
                w = (f"w1a{W1CH - 1}",)
            else:
                w = ("farw",)
            gd_dma(lambda e, ch=ch, wp=wp: e.dma_start(
                out=w_f4[wp][:], in_=ap["W2"][
                    (ch // W2CQ) * 128:(ch // W2CQ + 1) * 128,
                    (ch % W2CQ) * 1024:(ch % W2CQ + 1) * 1024]), w, f"w2l{ch}", stream="LW")
            s_op(lambda e, ch=ch, wp=wp: e.activation(
                sq_scr[:], w_f4[wp][:, 0:1024], AF.Abs,
                accum_out=wabs[:, 32 + ch:33 + ch]), (f"w2l{ch}",), f"w2a{ch}")

        def c2_ops():
            v_op(lambda e: e.tensor_reduce(wred[:, 1:2], wabs[:, 32:32 + W2CH], AX,
                                           ALU.add), (f"w2a{W2CH - 1}",), "w2r")
            t_op(lambda e: e.matmul(ps_d[0:1, 1:2], ones_fc[:], wred[:, 1:2],
                                    start=True, stop=True), ("w2r",), "w2t")
            v_op(lambda e: e.tensor_scalar(cscal[0:1, 1:2], ps_d[0:1, 1:2],
                                           0.5 / (FF * D), 5e-9, ALU.mult, ALU.max),
                 ("w2t",), "w2c")
            t_op(lambda e: e.matmul(ps_d[0:128, 2:3], onesc[0:1, 0:128],
                                    cscal[0:1, 1:2], start=True, stop=True),
                 ("w2c",), "w2cb")
            v_op(lambda e: e.tensor_copy(pc2[:, 9:10], ps_d[0:128, 2:3]), ("w2cb",), "c2")
            v_op(lambda e: e.tensor_scalar(pc2[:, 10:11], pc2[:, 9:10], -1.0, None,
                                           ALU.mult), ("c2",), "nc2")

        def wp2_chunk(pref, ch, srcfn, dstfn, cpos, ncpos, firstw, cw):
            wp = ch % 4
            wp2 = ch % 2
            w = [f"{pref}v{ch - 4}"] if ch >= 4 else firstw
            gd_dma(lambda e, ch=ch, wp=wp: e.dma_start(
                out=w_f4[wp][:, 0:cw], in_=srcfn(ch)), w, f"{pref}m{ch}", stream="LW")
            wg = [f"{pref}m{ch}"]
            if ch >= 2:
                wg.append(f"{pref}v{ch - 2}")
            v_op(lambda e, ch=ch, wp=wp, wp2=wp2: e.tensor_scalar(
                w_le[wp2][:, 0:cw], w_f4[wp][:, 0:cw], pc2[:, ncpos:ncpos + 1], None,
                ALU.is_le), wg, f"{pref}g{ch}")
            wv_ = [f"{pref}g{ch}"]
            if ch >= 2:
                wv_.append(f"{pref}s{ch - 2}")
            v_op(lambda e, ch=ch, wp=wp, wp2=wp2: e.scalar_tensor_tensor(
                w_tb[wp2][:, 0:cw], w_f4[wp][:, 0:cw], pc2[:, cpos:cpos + 1],
                w_le[wp2][:, 0:cw], ALU.is_ge, ALU.subtract), wv_, f"{pref}v{ch}")
            st_dma(lambda e, ch=ch, wp2=wp2: e.dma_start(
                out=dstfn(ch), in_=w_tb[wp2][:, 0:cw]),
                (f"{pref}v{ch}",), f"{pref}s{ch}", stream="SW")

        w1src = lambda ch: ap["W1"][ch * 128:(ch + 1) * 128, :]
        w1dst = lambda ch: w1t_d[ch * 128:(ch + 1) * 128, :]
        w2src = lambda ch: ap["W2"][(ch // W2CQ) * 128:(ch // W2CQ + 1) * 128,
                                    (ch % W2CQ) * 1024:(ch % W2CQ + 1) * 1024]
        w2dst = lambda ch: w2t_d[(ch // W2CQ) * 128:(ch // W2CQ + 1) * 128,
                                 (ch % W2CQ) * 1024:(ch % W2CQ + 1) * 1024]

        def bias_ap(q, E, g):
            G = E // 4
            if g < G - 2:
                return farw[:]
            isA = q >= M
            pos = 4 * g - (E - 8)
            base = 0 if isA else 8
            return bnd[:, (base + pos) * 128:(base + pos + 4) * 128]

        w2p1_i = 0
        w1p2_i = 0
        w2p2_i = 0
        ps_sim = [ps_a, ps_b, ps_c]
        HIGH_LAST_USER = {}
        LOW_LAST = {}
        att_st = xr_buf.bitcast(BF16)
        gx = 0
        GH = NT // 8

        chunkq = []
        for ch in range(W2CH):
            chunkq.append(lambda ch=ch: w2p1_chunk(ch))
            if ch == W2CH - 1:
                chunkq.append(c2_ops)
        for ch in range(W1CH):
            chunkq.append(lambda ch=ch: wp2_chunk(
                "w1", ch, w1src, w1dst, 7, 8, ("nc1", f"w2a{W2CH - 1}"), D))
        for ch in range(W2CH):
            chunkq.append(lambda ch=ch: wp2_chunk(
                "w2", ch, w2src, w2dst, 9, 10, ("nc2", f"w1v{W1CH - 1}"), 1024))
        NCHQ = len(chunkq)
        TOTU = sum(e // 4 for e in exts0)
        cq_i = [0]

        def drip(target):
            while cq_i[0] < min(target, NCHQ):
                chunkq[cq_i[0]]()
                cq_i[0] += 1

        for p in range(OWN // 2):
            qa, qb = 2 * p, 2 * p + 1
            gx0 = gx
            units = []
            for g in range(max(exts0[qa], exts0[qb]) // 4):
                for q in (qa, qb):
                    if g < exts0[q] // 4:
                        units.append((g, q))
                        if g < GH:
                            LOW_LAST[g] = gx + len(units) - 1
            U = len(units)
            firstu, lastu, glast = {}, {}, {}
            for ui, (g, q) in enumerate(units):
                firstu.setdefault(q, ui)
                lastu[q] = ui
                glast[g] = ui
            for g in range(GH, max(glast) + 1):
                hw = [f"svh{4 * g + 3}", f"svh{4 * g + 2}"]
                lu = HIGH_LAST_USER.get(g % 2)
                if lu is not None:
                    hw.append(f"tag{lu}")
                elif p == 0:
                    hw.append(f"vs{NT - 1}")
                HIGH_LAST_USER[g % 2] = gx0 + glast[g]
                j0 = 4 * g - HBASE
                ld_dma(lambda e, g=g, j0=j0: e.dma_start(
                    out=wvh[:, (g % 2) * 4 * D:(g % 2 + 1) * 4 * D],
                    in_=vhi_d[:, j0 * D:(j0 + 4) * D]),
                    hw, f"vh{p}_{g}", stream="LV")
            for ui in range(U + 1):
                if ui < U:
                    g, q = units[ui]
                    gi = gx0 + ui
                    gb = gi % 2
                    gp4 = gi % 3
                    wsim = []
                    if p == 0 and ui == 0:
                        wsim += [f"kaf{NGC - 1}", f"qaf{NG - 1}", "farw", "nc1"]
                    if gi >= 3:
                        wsim.append(f"vbg{gi - 3}")
                    for r in range(4):
                        t_op(lambda e, g=g, gp4=gp4, r=r, q=q: e.matmul(
                            ps_sim[gp4][:, r * 128:(r + 1) * 128],
                            kT[:, (4 * g + r) * 128:(4 * g + r + 1) * 128],
                            qT[:, q * 128:(q + 1) * 128], start=True, stop=True),
                            wsim if r == 0 else (), f"tsg{gi}" if r == 3 else None)
                if ui >= 1:
                    gg, qq = units[ui - 1]
                    gj = gx0 + ui - 1
                    ggb = gj % 2
                    po = qq % 2
                    wav = [f"vag{gj}"]
                    if gg >= GH:
                        wav.append(f"vh{p}_{gg}")
                    first = (ui - 1 == firstu[qq])
                    islast = (ui - 1 == lastu[qq])
                    if first:
                        if qq >= 2:
                            wav += [f"se{qq - 2}", f"vr{qq - 2}"]
                        else:
                            wav.append(f"vs{NT - 1}")
                    for r in range(4):
                        if gg < GH:
                            vsl = lambda r, half, gg=gg: vw[
                                :, (4 * gg + r) * D + half * 512:
                                (4 * gg + r) * D + (half + 1) * 512]
                        else:
                            vsl = lambda r, half, gg=gg: wvh[
                                :, (gg % 2) * 4 * D + r * D + half * 512:
                                (gg % 2) * 4 * D + r * D + (half + 1) * 512]
                        for half in range(HD):
                            st_ = first and r == 0
                            sp_ = (r == 3) and (half == HD - 1)
                            sp2_ = (r == 3) and islast
                            t_op(lambda e, ggb=ggb, r=r, half=half, po=po,
                                 st_=st_, sp2_=sp2_, vsl=vsl: e.matmul(
                                     ps_o[po][:, half * 512:(half + 1) * 512],
                                     attn2[ggb][:, r * 128:(r + 1) * 128],
                                     vsl(r, half),
                                     start=st_, stop=sp2_),
                                 wav if (r == 0 and half == 0) else (),
                                 f"tag{gj}" if sp_ else None)
                if ui < U:
                    wb = [f"tsg{gi}"]
                    if gi >= 2:
                        wb.append(f"seg{gi - 2}")
                    bs = bias_ap(q, exts0[q], g)
                    v_op(lambda e, gp4=gp4, gb=gb, bs=bs: e.tensor_tensor(
                        simb[gb][:], ps_sim[gp4][:], bs, ALU.add), wb, f"vbg{gi}")
                    we = [f"vbg{gi}"]
                    if gi >= 2:
                        we.append(f"vag{gi - 2}")
                    s_op(lambda e, gb=gb: e.activation(
                        attn_b[gb][:], simb[gb][:], AF.Erf, bias=pc2[:, 6:7],
                        scale=pc2[:, 5:6]), we, f"seg{gi}")
                    wa = [f"seg{gi}"]
                    if gi >= 2:
                        wa.append(f"tag{gi - 2}")
                    v_op(lambda e, gb=gb: e.tensor_scalar(
                        attn2[gb][:], attn_b[gb][:], 0.5, 0.5, ALU.mult, ALU.add),
                        wa, f"vag{gi}")
                    drip(((gi + 1) * 3 * NCHQ) // (2 * TOTU) + 1)
            gx = gx0 + U
            for q in (qa, qb):
                po = q % 2
                gl = gx0 + lastu[q]
                s_op(lambda e, q=q, po=po: e.activation(
                    sq_scr[:], ps_o[po][:, 0:D], AF.Square,
                    accum_out=st2[:, OWN + q:OWN + q + 1]), (f"tag{gl}",), f"sr{q}")
                wev = [f"sr{q}"]
                if q >= 2:
                    wev.append(f"sat{q - 2}")
                s_op(lambda e, q=q, po=po: e.activation(
                    att_st[:, po * 2 * D:po * 2 * D + D], ps_o[po][:, 0:D],
                    AF.Identity, accum_out=st2[:, q:q + 1]), wev, f"se{q}")
                v_op(lambda e: e.nop(), (f"se{q}",), f"vr{q}")
                td_dma(lambda e, q=q, po=po: e.dma_start(
                    out=att_d[q * 128:(q + 1) * 128, :],
                    in_=att_st[:, po * 2 * D:po * 2 * D + D]),
                    (f"se{q}",), f"sat{q}", stream="SA")
        drip(NCHQ)

        lastq = OWN - 1
        GXT = sum(e // 4 for e in exts0)
        for dc in range(DC):
            wv_ = [f"w1s{W1CH - 1}", f"w1s{W1CH - 2}"] if dc == 0 else []
            if dc in LOW_LAST:
                wv_.append(f"tag{LOW_LAST[dc]}")
            td_dma(lambda e, dc=dc: e.dma_start_transpose(
                vw[:, dc * FF:(dc + 1) * FF], w1t_d[:, dc * 128:(dc + 1) * 128]),
                wv_, f"tw1_{dc}", stream="TW")

        def rstd_batch(stt, wlist, tag):
            v_op(lambda e, stt=stt: e.tensor_scalar(
                stt[:, 3 * OWN:4 * OWN], stt[:, 0:OWN], 1.0 / D, None, ALU.mult),
                wlist, f"{tag}m")
            v_op(lambda e, stt=stt: e.tensor_scalar(
                stt[:, OWN:2 * OWN], stt[:, OWN:2 * OWN], 1.0 / D, None, ALU.mult),
                wlist, f"{tag}e")
            v_op(lambda e, stt=stt: e.tensor_tensor(
                stt[:, 0:OWN], stt[:, 3 * OWN:4 * OWN], stt[:, 3 * OWN:4 * OWN],
                ALU.mult), (f"{tag}m",), f"{tag}s")
            v_op(lambda e, stt=stt: e.tensor_tensor(
                stt[:, OWN:2 * OWN], stt[:, OWN:2 * OWN], stt[:, 0:OWN], ALU.subtract),
                (f"{tag}s", f"{tag}e"), f"{tag}v")
            s_op(lambda e, stt=stt: e.activation(
                stt[:, 2 * OWN:3 * OWN], stt[:, OWN:2 * OWN], AF.Sqrt,
                bias=pc2[:, 4:5], scale=pc2[:, 11:12]), (f"{tag}v",), f"{tag}q")
            v_op(lambda e, stt=stt: e.reciprocal(
                stt[:, 2 * OWN:3 * OWN], stt[:, 2 * OWN:3 * OWN]), (f"{tag}q",), f"{tag}r")
            v_op(lambda e, stt=stt: e.scalar_tensor_tensor(
                stt[:, 3 * OWN:4 * OWN], stt[:, 3 * OWN:4 * OWN], -1.0,
                stt[:, 2 * OWN:3 * OWN], ALU.mult, ALU.mult), (f"{tag}r",), f"{tag}n")

        rstd_batch(st2, (f"sr{OWN - 1}", f"vr{OWN - 1}"), "r2")

        NG3 = OWN // 4 if OWN >= 4 else 1
        TPG = min(4, OWN)
        psy = [(ps_o0[:, 0:512], ps_o0[:, 512:1024]),
               (ps_o1[:, 0:512], ps_o1[:, 512:1024]),
               (ps_a[:, 0:512], ps_b[:, 0:512]),
               (ps_c[:, 0:512], ps_d[:, 0:512])]
        lastq = OWN - 1
        GXT2 = sum(e // 4 for e in exts0)

        def emit_ffn1(g3):
            gp = (g3 + 1) % 2
            for fc in range(FC):
                fp = fc % 2
                wf = [f"t3_{g3}_{DC - 1}", f"t3_{g3}_{DC - 2}"]
                if fc == 0 and g3 == 0:
                    wf += [f"tw1_{DC - 1}", f"tw1_{DC - 2}", f"tw1_{DC - 3}",
                           f"tw1_{DC - 4}", f"vbg{GXT2 - 1}",
                           f"w2s{W2CH - 1}", f"w2s{W2CH - 2}", "nc2"]
                if fc >= 2:
                    wf.append(f"gl{g3}_{fc - 2}")
                elif g3 >= 1:
                    wf.append(f"gl{g3 - 1}_{FC - 1}")
                if g3 >= 1 and fc < 2:
                    wf += [f"sey_{g3 * TPG - 1}", f"v4b{g3 * TPG - 1}"]
                for dc in range(DC):
                    t_op(lambda e, gp=gp, fc=fc, fp=fp, dc=dc: e.matmul(
                        ps_cd[fp][:],
                        vw[:, dc * FF + fc * 128:dc * FF + (fc + 1) * 128],
                        xnT[:, gp * DC * 512 + dc * 512:gp * DC * 512 + (dc + 1) * 512],
                        start=(dc == 0), stop=(dc == DC - 1)),
                        wf if dc == 0 else (), f"f1_{g3}_{fc}" if dc == DC - 1 else None)
                wg = [f"f1_{g3}_{fc}"]
                if g3 >= 1 and fc < 2:
                    wg.append(f"t2_{g3 - 1}_{FC - 1}")
                if g3 == 0 and fc < 2:
                    wg += [f"w2s{W2CH - 1}", f"w2s{W2CH - 2}"]
                s_op(lambda e, fc=fc, fp=fp: e.activation(
                    hT[:, fc * 512:(fc + 1) * 512], ps_cd[fp][:], AF.Gelu,
                    bias=pcols[:, 5 + fc:5 + fc + 1]), wg, f"gl{g3}_{fc}")

        def emit_ffn2(g3):
            for fc in range(FC):
                wl = []
                fi = g3 * FC + fc
                if fi >= 2:
                    wl.append(f"t2_{g3}_{fc - 2}" if fc >= 2
                              else f"t2_{g3 - 1}_{FC - (2 - fc)}")
                if fi == 0:
                    wl += [f"w2s{W2CH - 1}", f"w2s{W2CH - 2}"]
                td_dma(lambda e, fc=fc: e.dma_start_transpose(
                    w2w[:, (fc % 2) * D:(fc % 2 + 1) * D],
                    w2t_d[:, fc * 128:(fc + 1) * 128]), wl, f"t2l{g3}_{fc}",
                    stream="TW")
                for tt in range(TPG):
                    wm = [f"t2l{g3}_{fc}"]
                    if fc == 0:
                        wm.append(f"gl{g3}_{FC - 1}")
                        q4 = g3 * TPG + tt
                        if g3 == 0:
                            if tt < 2:
                                wm += [f"se{OWN - 2 + tt}", f"vr{OWN - 2 + tt}"]
                        else:
                            wm += [f"sey_{q4 - TPG}", f"v4b{q4 - TPG}"]
                    for half in range(2):
                        t_op(lambda e, g3=g3, fc=fc, tt=tt, half=half: e.matmul(
                            psy[tt][half],
                            hT[:, fc * 512 + tt * 128:fc * 512 + tt * 128 + 128],
                            w2w[:, (fc % 2) * D + half * 512:
                                (fc % 2) * D + (half + 1) * 512],
                            start=(fc == 0), stop=False),
                            wm if half == 0 else (),
                            f"t2_{g3}_{fc}" if (tt == TPG - 1 and half == 1) else None)
            for tt in range(TPG):
                q4 = g3 * TPG + tt
                for half in range(2):
                    t_op(lambda e, tt=tt, half=half: e.matmul(
                        psy[tt][half], ones_bfr[:],
                        b2r[0:1, half * 512:(half + 1) * 512], start=False, stop=True),
                        ("b2r",) if (tt == 0 and half == 0) else (),
                        f"tb2_{g3}_{tt}" if half == 1 else None)
                wsq0 = [f"tb2_{g3}_{tt}"]
                if q4 >= 1:
                    wsq0.append(f"v4b{q4 - 1}")
                s_op(lambda e, tt=tt: e.activation(
                    sq_scr[:, 0:512], psy[tt][0], AF.Square,
                    accum_out=st4[:, 4 * OWN:4 * OWN + 1]), wsq0, None)
                s_op(lambda e, tt=tt: e.activation(
                    sq_scr[:, 512:1024], psy[tt][1], AF.Square,
                    accum_out=st4[:, 4 * OWN + 1:4 * OWN + 2]), (), f"s4h{q4}")
                p2 = q4 % 2
                wev = [f"s4h{q4}"]
                if q4 >= 1:
                    wev.append(f"v4a{q4 - 1}")
                if q4 >= 2:
                    wev.append(f"sy{q4 - 2}")
                else:
                    wev += [f"svh{NT - 1}", f"svh{NT - 2}"]
                s_op(lambda e, tt=tt, p2=p2: e.activation(
                    vstg[:, p2 * D:p2 * D + 512], psy[tt][0], AF.Identity,
                    accum_out=st4[:, 4 * OWN + 2:4 * OWN + 3]), wev, None)
                s_op(lambda e, tt=tt, p2=p2: e.activation(
                    vstg[:, p2 * D + 512:(p2 + 1) * D], psy[tt][1], AF.Identity,
                    accum_out=st4[:, 4 * OWN + 3:4 * OWN + 4]), (), f"se4c{q4}")
                v_op(lambda e, q4=q4: e.tensor_tensor(
                    st4[:, q4:q4 + 1], st4[:, 4 * OWN + 2:4 * OWN + 3],
                    st4[:, 4 * OWN + 3:4 * OWN + 4], ALU.add),
                    (f"se4c{q4}",), f"v4a{q4}")
                v_op(lambda e, q4=q4: e.tensor_tensor(
                    st4[:, OWN + q4:OWN + q4 + 1], st4[:, 4 * OWN:4 * OWN + 1],
                    st4[:, 4 * OWN + 1:4 * OWN + 2], ALU.add),
                    (f"s4h{q4}",), f"v4b{q4}")
                s_op(lambda e: e.nop(), (f"v4b{q4}",), f"sey_{q4}")
                td_dma(lambda e, q4=q4, p2=p2: e.dma_start(
                    out=y_d[q4 * 128:(q4 + 1) * 128, :],
                    in_=vstg[:, p2 * D:(p2 + 1) * D]), (f"sey_{q4}",), f"sy{q4}",
                    stream="SY")

        for q in range(OWN):
            p2 = q % 2
            w = [f"sat{q}"]
            if q >= 2:
                w.append(f"n2_{q - 2}")
            else:
                w += [f"w2s{W2CH - 1}", f"w2s{W2CH - 2}"]
            ld_dma(lambda e, q=q, p2=p2: e.dma_start(
                out=x3bf[:, p2 * 2 * D:p2 * 2 * D + D],
                in_=att_d[q * 128:(q + 1) * 128, :]), w, f"lat{q}", stream="LA")
            w = []
            if q >= 2:
                w.append(f"n3_{q - 2}")
            else:
                w.append(f"sat{OWN - 2 + q}")
            ld_dma(lambda e, q=q, p2=p2: e.dma_start(
                out=xr_buf[:, p2 * D:(p2 + 1) * D],
                in_=ap["xcov"][(NT + q) * 128:(NT + q + 1) * 128, :]), w, f"lxo{q}", stream="LX")
            wn = [f"lat{q}", "r2n"]
            if q >= 2:
                wn.append(f"vx2_{q - 2}")
            else:
                wn.append(f"sat{OWN - 2 + q}")
            s_op(lambda e, q=q, p2=p2: e.activation(
                evac[:, p2 * D:(p2 + 1) * D], x3bf[:, p2 * 2 * D:p2 * 2 * D + D],
                AF.Identity, bias=st2[:, 3 * OWN + q:3 * OWN + q + 1],
                scale=st2[:, 2 * OWN + q:2 * OWN + q + 1]), wn, f"n2_{q}")
            v_op(lambda e, q=q, p2=p2: e.tensor_tensor(
                xr_buf[:, p2 * D:(p2 + 1) * D], evac[:, p2 * D:(p2 + 1) * D],
                xr_buf[:, p2 * D:(p2 + 1) * D], ALU.add),
                (f"n2_{q}", f"lxo{q}"), f"vx2_{q}")
            st_dma(lambda e, q=q, p2=p2: e.dma_start(
                out=x2_d[q * 128:(q + 1) * 128, :], in_=xr_buf[:, p2 * D:(p2 + 1) * D]),
                (f"vx2_{q}",), f"sx2_{q}", stream="SX")
            s_op(lambda e, q=q, p2=p2: e.activation(
                sq_scr[:], xr_buf[:, p2 * D:(p2 + 1) * D], AF.Square,
                accum_out=st3[:, OWN + q:OWN + q + 1]), (f"vx2_{q}",), f"s3q{q}")
            v_op(lambda e, q=q, p2=p2: e.tensor_reduce(
                st3[:, q:q + 1], xr_buf[:, p2 * D:(p2 + 1) * D], AX, ALU.add),
                (f"vx2_{q}",), f"v3x{q}")
            v_op(lambda e, q=q: e.tensor_scalar(
                st3[:, 3 * OWN + q:3 * OWN + q + 1], st3[:, q:q + 1], 1.0 / D, None,
                ALU.mult), (f"v3x{q}",), f"m3_{q}")
            v_op(lambda e, q=q: e.tensor_scalar(
                st3[:, OWN + q:OWN + q + 1], st3[:, OWN + q:OWN + q + 1], 1.0 / D, None,
                ALU.mult), (f"s3q{q}",), f"e3_{q}")
            v_op(lambda e, q=q: e.tensor_tensor(
                st3[:, q:q + 1], st3[:, 3 * OWN + q:3 * OWN + q + 1],
                st3[:, 3 * OWN + q:3 * OWN + q + 1], ALU.mult), (f"m3_{q}",), f"ms3_{q}")
            v_op(lambda e, q=q: e.tensor_tensor(
                st3[:, OWN + q:OWN + q + 1], st3[:, OWN + q:OWN + q + 1],
                st3[:, q:q + 1], ALU.subtract), (f"ms3_{q}", f"e3_{q}"), f"v3_{q}")
            s_op(lambda e, q=q: e.activation(
                st3[:, 2 * OWN + q:2 * OWN + q + 1], st3[:, OWN + q:OWN + q + 1],
                AF.Sqrt, bias=pc2[:, 4:5], scale=pc2[:, 11:12]), (f"v3_{q}",), f"q3_{q}")
            v_op(lambda e, q=q: e.reciprocal(
                st3[:, 2 * OWN + q:2 * OWN + q + 1],
                st3[:, 2 * OWN + q:2 * OWN + q + 1]), (f"q3_{q}",), f"r3_{q}")
            v_op(lambda e, q=q: e.scalar_tensor_tensor(
                st3[:, 3 * OWN + q:3 * OWN + q + 1],
                st3[:, 3 * OWN + q:3 * OWN + q + 1], -1.0,
                st3[:, 2 * OWN + q:2 * OWN + q + 1], ALU.mult, ALU.mult),
                (f"r3_{q}",), f"n3c_{q}")
            w4 = [f"n3c_{q}"]
            if q >= 2:
                w4.append(f"sn3_{q - 2}")
            s_op(lambda e, q=q, p2=p2: e.activation(
                evbf[:, p2 * 2 * D + D:p2 * 2 * D + 2 * D],
                xr_buf[:, p2 * D:(p2 + 1) * D], AF.Identity,
                bias=st3[:, 3 * OWN + q:3 * OWN + q + 1],
                scale=st3[:, 2 * OWN + q:2 * OWN + q + 1]), w4, f"n3_{q}")
            td_dma(lambda e, q=q, p2=p2: e.dma_start(
                out=xn3_d[q * 128:(q + 1) * 128, :],
                in_=evbf[:, p2 * 2 * D + D:p2 * 2 * D + 2 * D]), (f"n3_{q}",),
                f"sn3_{q}", stream="SN")
            if (q + 1) % TPG == 0:
                g3 = q // TPG
                gp = (g3 + 1) % 2
                for dc in range(DC):
                    w5 = [f"sn3_{q}", f"sn3_{q - 1}"]
                    if g3 == 1:
                        w5.append(f"tag{TOTU - 1}")
                    if g3 >= 2:
                        w5.append(f"f1_{g3 - 2}_{FC - 1}")
                    if g3 == 0:
                        ld_dma(lambda e, g3=g3, gp=gp, dc=dc: e.dma_start_transpose(
                            xnT[:, gp * DC * 512 + dc * 512:
                                gp * DC * 512 + (dc + 1) * 512],
                            xn3_d[g3 * TPG * 128:(g3 + 1) * TPG * 128,
                                  dc * 128:(dc + 1) * 128]),
                            w5, f"t3_{g3}_{dc}", stream="T3")
                    else:
                        td_dma(lambda e, g3=g3, gp=gp, dc=dc: e.dma_start_transpose(
                            xnT[:, gp * DC * 512 + dc * 512:
                                gp * DC * 512 + (dc + 1) * 512],
                            xn3_d[g3 * TPG * 128:(g3 + 1) * TPG * 128,
                                  dc * 128:(dc + 1) * 128]),
                            w5, f"t3_{g3}_{dc}", stream="T5")
                if g3 >= 1:
                    emit_ffn2(g3 - 1)
                emit_ffn1(g3)
        emit_ffn2(NG3 - 1)

        rstd_batch(st4, (f"s4h{OWN - 1}", f"v4b{OWN - 1}"), "r4")
        for q in range(OWN):
            p2 = q % 2
            w = ["r4n", f"sy{q}"]
            if q >= 2:
                w.append(f"so{q - 2}")
            ld_dma(lambda e, q=q, p2=p2: e.dma_start(
                out=x3bf[:, p2 * 2 * D:p2 * 2 * D + D],
                in_=y_d[q * 128:(q + 1) * 128, :]), w, f"ly{q}", stream="LA")
            w = [f"sx2_{q}"]
            if q >= 2:
                w.append(f"vo{q - 2}")
            ld_dma(lambda e, q=q, p2=p2: e.dma_start(
                out=xr_buf[:, p2 * D:(p2 + 1) * D],
                in_=x2_d[q * 128:(q + 1) * 128, :]), w, f"lx2{q}", stream="LX")
            wn = [f"ly{q}", "r4n"]
            if q >= 2:
                wn.append(f"vo{q - 2}")
            else:
                wn.append(f"sn3_{OWN - 2 + q}")
            s_op(lambda e, q=q, p2=p2: e.activation(
                evac[:, p2 * D:(p2 + 1) * D], x3bf[:, p2 * 2 * D:p2 * 2 * D + D],
                AF.Identity, bias=st4[:, 3 * OWN + q:3 * OWN + q + 1],
                scale=st4[:, 2 * OWN + q:2 * OWN + q + 1]), wn, f"n4_{q}")
            v_op(lambda e, q=q, p2=p2: e.tensor_tensor(
                x_in[:, p2 * D:(p2 + 1) * D], evac[:, p2 * D:(p2 + 1) * D],
                xr_buf[:, p2 * D:(p2 + 1) * D], ALU.add),
                (f"n4_{q}", f"lx2{q}"), f"vo{q}")
            st_dma(lambda e, q=q, p2=p2: e.dma_start(
                out=out_hbm[q * 128:(q + 1) * 128, :],
                in_=x_in[:, p2 * D:(p2 + 1) * D]), (f"vo{q}",), f"so{q}", stream="SO")

        if trunc is not None:
            del SCHED[int(len(SCHED) * trunc):]
        counts = {k: 0 for k in SEMS}
        EV = {}
        for eng, fn, waits, sig, semkey, inc, selfw in SCHED:
            if semkey != "?":
                counts[semkey] = counts.get(semkey, 0) + inc
                if sig is not None:
                    assert sig not in EV, f"duplicate event name {sig}"
                    EV[sig] = (semkey, counts[semkey])

        block = ctx.enter_context(nc.Block())

        def emit_for(eng_name, engine):
            hwm = {}

            def w(sk, val):
                if val <= 0 or hwm.get(sk, 0) >= val:
                    return
                hwm[sk] = val
                engine.wait_ge(SEMS[sk], val)

            for eng, fn, waits, sig, semkey, inc, selfw in SCHED:
                if eng != eng_name:
                    continue
                for wname in waits:
                    sk, val = EV[wname]
                    w(sk, val)
                if selfw is not None:
                    w(*selfw)
                ins = fn(engine)
                if semkey != "?":
                    ins.then_inc(SEMS[semkey], inc)
            if eng_name == "gpsimd":
                for sk, val in counts.items():
                    if sk not in ("S", "V", "T", "G") and val > 0:
                        w(sk, val)

        @block.sync
        def _(e):
            emit_for("sync", e)

        @block.scalar
        def _(e):
            emit_for("scalar", e)

        @block.vector
        def _(e):
            emit_for("vector", e)

        @block.tensor
        def _(e):
            emit_for("tensor", e)

        @block.gpsimd
        def _(e):
            emit_for("gpsimd", e)

    return nc


def make_in_maps(inputs, NT, D, FF):
    x = np.asarray(inputs["L0_attn_x"], np.float32)
    B = x.shape[0]
    planes = _host_planes()
    eyef = np.eye(128, dtype=np.float32)
    eyeb = np.eye(128).astype(ml_dtypes.bfloat16)
    wqk = np.ascontiguousarray(np.asarray(inputs["L0_attn_wqk"], np.float32))
    bqk = np.asarray(inputs["L0_attn_bqk"], np.float32)
    gamma = np.asarray(inputs["L0_attn_gamma"], np.float32)
    beta = np.asarray(inputs["L0_attn_beta"], np.float32)
    wv = np.ascontiguousarray(np.asarray(inputs["L0_attn_wv"], np.float32))
    bv = np.asarray(inputs["L0_attn_bv"], np.float32)
    relb = np.asarray(inputs["L0_attn_rel_bias"], np.float32)
    W1 = np.ascontiguousarray(np.asarray(inputs["L0_ffn_W1"], np.float32))
    b1 = np.asarray(inputs["L0_ffn_b1"], np.float32)
    W2 = np.ascontiguousarray(np.asarray(inputs["L0_ffn_W2"], np.float32))
    b2 = np.asarray(inputs["L0_ffn_b2"], np.float32)
    FC = FF // 128
    DCC = D // 128
    ppack = np.concatenate([bqk[None, :], gamma[0][None, :], beta[0][None, :],
                            gamma[1][None, :], beta[1][None, :],
                            b1.reshape(FC, 128), b2.reshape(DCC, 128)],
                           axis=0).astype(np.float32)
    in_maps, metas = [], []
    for c in range(8):
        g, k = (c // 4) % B, c % 4
        tiles, _ = _own_tiles(k, NT)
        xb = x[g]
        xown = np.concatenate([xb[t * 128:(t + 1) * 128] for t in tiles], axis=0)
        xcov = np.ascontiguousarray(np.concatenate([xb, xown], axis=0))
        in_maps.append({
            "xcov": xcov, "wqk": wqk, "wv": wv, "bv": bv, "relb": relb,
            "W1": W1, "W2": W2, "b2v": b2, "ppack": ppack, "planes": planes,
            "ohbnd": _host_ohbnd(k), "eyef": eyef,
            "eyeb": eyeb,
        })
        metas.append((g, tiles))
    return in_maps, metas


def kernel(**inputs):
    NT, D, FF = 64, 1024, 4096
    x = np.asarray(inputs["L0_attn_x"], np.float32)
    nc = build_nc(NT, D, FF)
    in_maps, metas = make_in_maps(inputs, NT, D, FF)
    res = run_bass_kernel_spmd(nc, in_maps, core_ids=list(range(8)))
    out = np.zeros_like(x)
    for c in range(8):
        g, tiles = metas[c]
        r = res.results[c]["out"]
        for i, t in enumerate(tiles):
            out[g, t * 128:(t + 1) * 128, :] = r[i * 128:(i + 1) * 128, :]
    return out


# version 23
# speedup vs baseline: 1.1624x; 1.1624x over previous
import math
import contextlib
import numpy as np
import ml_dtypes
import concourse.bass as bass
import concourse.mybir as mybir
from concourse.bass_utils import run_bass_kernel_spmd

F32 = mybir.dt.float32
BF16 = mybir.dt.bfloat16
AF = mybir.ActivationFunctionType
ALU = mybir.AluOpType
AX = mybir.AxisListType.X

ERF_SCALE = float(np.float32(1.0 / math.sqrt(math.pi / 2.0)))
ERF_BIAS = float(np.float32(-math.sqrt(0.5) / math.sqrt(math.pi / 2.0)))
SQRT_DQK = float(np.float32(math.sqrt(128.0)))
MASKVAL = -30000.0


def _bucket_of(n):
    n = np.asarray(n)
    n_safe = np.maximum(n, 1).astype(np.float32)
    v = np.log(n_safe / np.float32(16.0)).astype(np.float32)
    v = (v / np.float32(math.log(8.0))).astype(np.float32)
    v = (v * np.float32(16.0)).astype(np.float32)
    vl = np.minimum(16 + v.astype(np.int32), 31)
    return np.where(n < 16, n, vl)


def _host_planes():
    p = np.arange(128)[:, None]
    f = np.arange(128)[None, :]
    d0 = f - p
    b0 = _bucket_of(np.maximum(d0, 0))
    tri = np.zeros((128, 33 * 128), np.float32)
    for b in range(32):
        tri[:, b * 128:(b + 1) * 128] = ((b0 == b) & (d0 >= 0)).astype(np.float32)
    tri[:, 32 * 128:] = (d0 < 0).astype(np.float32)
    d1 = 128 + f - p
    b1 = _bucket_of(d1)
    dd1 = np.zeros((128, 33 * 128), np.float32)
    for b in range(32):
        dd1[:, b * 128:(b + 1) * 128] = (b1 == b).astype(np.float32)
    return np.concatenate([tri, dd1], axis=1)


def _gsel(delta):
    if delta >= 2:
        return 2
    if delta == 1:
        return 1
    if delta == 0:
        return 0
    return 3


def _host_ohbnd(k):
    oh = np.zeros((16, 4), np.float32)
    for pos in range(8):
        oh[pos, _gsel(k + 4 - pos)] = 1.0
        oh[8 + pos, _gsel(7 - k - pos)] = 1.0
    return np.repeat(oh.reshape(1, 64), 128, axis=0).astype(np.float32)


def _own_tiles(k, NT):
    M = NT // 8
    tiles = [NT - 1 - 4 * m - k for m in range(M)] + [4 * m + k for m in range(M)]
    exts = [NT - 4 * m for m in range(M)] + [4 * m + 4 for m in range(M)]
    return tiles, exts


def build_nc(NT, D, FF, trunc=None):
    S = NT * 128
    M = NT // 8
    OWN = 2 * M
    DC = D // 128
    FC = FF // 128
    NCT = NT + OWN
    NG = NCT // 4
    NGC = NT // 4
    HBASE = NT // 2
    INV_S = float(np.float32(1.0 / S))
    NPR = 5 + FC + DC
    W1CH = FF // 128
    W2CQ = FF // 1024
    W2CH = DC * W2CQ
    HD = D // 512
    _, exts0 = _own_tiles(0, NT)

    nc = bass.Bass("TRN2", target_bir_lowering=False)
    ap = {}

    def din(name, shape, dt=F32):
        ap[name] = nc.dram_tensor(name, shape, dt, kind="ExternalInput").ap()

    din("xcov", [NCT * 128, D])
    din("wqk", [D, 128])
    din("wv", [D, D])
    din("bv", [D])
    din("relb", [32])
    din("W1", [FF, D])
    din("W2", [D, FF])
    din("b2v", [D])
    din("ppack", [NPR, 128])
    din("planes", [128, 2 * 33 * 128])
    din("ohbnd", [128, 64])
    din("eyef", [128, 128])
    din("eyeb", [128, 128], BF16)
    out_hbm = nc.dram_tensor("out", [OWN * 128, D], F32, kind="ExternalOutput").ap()

    xn1_d = nc.dram_tensor("xn1_d", [NCT * 128, D], BF16, kind="Internal").ap()
    vhi_d = nc.dram_tensor("vhi_d", [128, (NT - HBASE) * D], BF16, kind="Internal").ap()
    att_d = nc.dram_tensor("att_d", [OWN * 128, D], BF16, kind="Internal").ap()
    x2_d = nc.dram_tensor("x2_d", [OWN * 128, D], F32, kind="Internal").ap()
    xn3_d = nc.dram_tensor("xn3_d", [OWN * 128, D], BF16, kind="Internal").ap()
    y_d = nc.dram_tensor("y_d", [OWN * 128, D], BF16, kind="Internal").ap()
    w1t_d = nc.dram_tensor("w1t_d", [FF, D], BF16, kind="Internal").ap()
    w2t_d = nc.dram_tensor("w2t_d", [D, FF], BF16, kind="Internal").ap()

    VWC = max(HBASE * D, DC * FF)
    WVHC = max(DC * D, 8 * D)

    ctx = contextlib.ExitStack()
    with ctx:
        sbt = lambda name, shape, dt, **kw: ctx.enter_context(
            nc.sbuf_tensor(name, shape, dt, **kw))

        vw = sbt("vw", [128, VWC], BF16, is_transpose_dest=True)
        wvh = sbt("wvh", [128, WVHC], BF16)
        kT = sbt("kT", [128, NT * 128], BF16)
        qT = sbt("qT", [128, OWN * 128], BF16)
        XNTC = max(2 * DC * 512, 6144)
        xnT = sbt("xnT", [128, XNTC], BF16, is_transpose_dest=True)
        war = sbt("war", [128, 16384], BF16, is_transpose_dest=True)
        x_in = sbt("x_in", [128, 2 * D], F32)
        xr_buf = sbt("xr_buf", [128, 2 * D], F32)
        evac = sbt("evac", [128, 2 * D], F32)
        sq_scr = sbt("sq_scr", [128, max(D, 1024)], F32)
        bnd = sbt("bnd", [128, 2048], F32)
        farw = sbt("farw", [128, 512], F32)
        s1x = sbt("s1x", [128, NCT], F32)
        s1q = sbt("s1q", [128, NCT], F32)
        r1 = sbt("r1", [128, NCT], F32)
        n1 = sbt("n1", [128, NCT], F32)
        st2 = sbt("st2", [128, 4 * OWN + 2], F32)
        st3 = sbt("st3", [128, 4 * OWN], F32)
        st4 = sbt("st4", [128, 4 * OWN + 4], F32)
        onesc = sbt("onesc", [128, 512], F32)
        vstg = sbt("vstg", [128, 2 * D], BF16)
        w2w = sbt("w2w", [128, 2 * D], BF16, is_transpose_dest=True)
        pstage = sbt("pstage", [128, 128], F32)
        eye_t = sbt("eye_t", [128, 128], F32)
        eye_b = sbt("eye_b", [128, 128], BF16)
        pcols = sbt("pcols", [128, NPR], F32)
        pc2 = sbt("pc2", [128, 12], F32)
        wabs = sbt("wabs", [128, 64], F32)
        wred = sbt("wred", [128, 2], F32)
        cscal = sbt("cscal", [1, 2], F32)
        relrow = sbt("relrow", [1, 33], F32)
        bvrow = sbt("bvrow", [1, D], BF16)
        b2r = sbt("b2r", [1, D], BF16)
        ones_bfr = sbt("ones_bfr", [1, 128], BF16)
        ones_fc = sbt("ones_fc", [128, 1], F32)
        plane_t = sbt("plane_t", [128, 6 * 128], F32)

        warf = war.bitcast(F32)
        w_f = [warf[:, 0:1024], warf[:, 1024:2048]]
        w_le = [warf[:, 2048:3072], warf[:, 3072:4096]]
        w_tb = [war[:, 8192:9216], war[:, 9216:10240]]
        qks = war[:, 10240:11264]
        wqk_b = war[:, 11264:11264 + DC * 128]
        masters = warf[:, 6144:6656]
        t33 = warf[:, 6656:6689]
        w_f4 = [warf[:, 0:1024], warf[:, 1024:2048],
                warf[:, 6144:7168], warf[:, 7168:8192]]
        hT = war[:, 0:FC * 512]
        wv_b = wvh[:, 0:DC * D]
        xnTf = xnT.bitcast(F32)
        simb = [xnTf[:, 0:512], xnTf[:, 512:1024]]
        attn_b = [xnT[:, 2048:2560], xnT[:, 2560:3072]]
        attn2 = [xnT[:, 3072:3584], xnT[:, 3584:4096]]
        xn_b = xr_buf.bitcast(BF16)
        x3bf = x_in.bitcast(BF16)
        evbf = evac.bitcast(BF16)

        ps_o0 = ctx.enter_context(nc.psum_tensor("ps_o0", [128, 1024], F32))
        ps_o1 = ctx.enter_context(nc.psum_tensor("ps_o1", [128, 1024], F32))
        ps_a = ctx.enter_context(nc.psum_tensor("ps_a", [128, 512], F32))
        ps_b = ctx.enter_context(nc.psum_tensor("ps_b", [128, 512], F32))
        ps_c = ctx.enter_context(nc.psum_tensor("ps_c", [128, 512], F32))
        ps_d = ctx.enter_context(nc.psum_tensor("ps_d", [128, 512], F32))
        ps_o = [ps_o0, ps_o1]
        ps_ab = [ps_a, ps_b]
        ps_cd = [ps_c, ps_d]

        SEMS = {k: ctx.enter_context(nc.semaphore(k)) for k in ("S", "V", "T", "G")}
        STRC = {}

        def _stream_sem(tag, par):
            key = f"{tag}{par}"
            if key not in SEMS:
                SEMS[key] = ctx.enter_context(nc.semaphore(key))
            return key

        SCHED = []

        def op(eng, fn, waits=(), sig=None, sem="?", inc=1, selfwait=None):
            SCHED.append((eng, fn, tuple(waits), sig, sem, inc, selfwait))

        def t_op(fn, waits=(), sig=None):
            op("tensor", fn, waits, sig, "T")

        def s_op(fn, waits=(), sig=None):
            op("scalar", fn, waits, sig, "S")

        def v_op(fn, waits=(), sig=None):
            op("vector", fn, waits, sig, "V")

        def g_op(fn, waits=(), sig=None):
            op("gpsimd", fn, waits, sig, "G")

        NPAR = {"LX": 4, "LN": 4, "LW": 4, "LV": 4, "T0": 4, "TW": 4, "S0": 4, "LA": 4}

        def _dma(eng, stream, fn, waits, sig):
            i = STRC.get(stream, 0)
            STRC[stream] = i + 1
            np_ = NPAR.get(stream, 2)
            semkey = _stream_sem(stream, i % np_)
            val = 16 * (i // np_ + 1)
            selfw = (semkey, val - 16) if val > 16 else None
            op(eng, fn, waits, sig, semkey, 16, selfw)

        def ld_dma(fn, waits=(), sig=None, stream="L0"):
            _dma("sync", stream, fn, waits, sig)

        def td_dma(fn, waits=(), sig=None, stream="T0"):
            _dma("scalar", stream, fn, waits, sig)

        def st_dma(fn, waits=(), sig=None, stream="S0"):
            _dma("gpsimd", stream, fn, waits, sig)

        def gd_dma(fn, waits=(), sig=None, stream="LN"):
            _dma("gpsimd", stream, fn, waits, sig)

        v_op(lambda e: e.memset(onesc[:], 1.0), (), "ones")
        v_op(lambda e: e.memset(ones_fc[:], 1.0), (), "onesfc")
        v_op(lambda e: e.memset(ones_bfr[:], 1.0), (), "onesbf")
        v_op(lambda e: e.memset(pstage[:], 0.0), (), "pstz")
        v_op(lambda e: e.memset(pc2[:, 4:5], 1e-5), (), "eps")
        v_op(lambda e: e.memset(pc2[:, 5:6], ERF_SCALE), (), "erfS")
        v_op(lambda e: e.memset(pc2[:, 6:7], ERF_BIAS), (), "erfB")
        v_op(lambda e: e.memset(pc2[:, 11:12], 1.0), (), "one1")
        ld_dma(lambda e: e.dma_start(out=eye_t[:], in_=ap["eyef"]), (), "eye")
        ld_dma(lambda e: e.dma_start(out=eye_b[:], in_=ap["eyeb"]), (), "eyb")
        ld_dma(lambda e: e.dma_start(out=pstage[0:NPR, :], in_=ap["ppack"]), ("pstz",), "ppk")
        ld_dma(lambda e: e.dma_start(
            out=relrow[:, 0:32], in_=ap["relb"].rearrange("(o t) -> o t", o=1)), (), "rel")
        ld_dma(lambda e: e.dma_start(out=farw[:, 0:64], in_=ap["ohbnd"]), (), "ohb")
        for dc in range(DC):
            ld_dma(lambda e, dc=dc: e.dma_start(
                out=w_f[dc % 2][:, 0:128],
                in_=ap["wqk"][dc * 128:(dc + 1) * 128, :]),
                (f"cwqk{dc - 2}",) if dc >= 2 else (), f"lwqk{dc}")
            v_op(lambda e, dc=dc: e.tensor_copy(
                wqk_b[:, dc * 128:(dc + 1) * 128], w_f[dc % 2][:, 0:128]),
                (f"lwqk{dc}",), f"cwqk{dc}")
        for dc in range(DC):
            ld_dma(lambda e, dc=dc: e.dma_start(
                out=w_f[dc % 2][:, 0:D], in_=ap["wv"][dc * 128:(dc + 1) * 128, :]),
                (f"cwv{dc - 2}",) if dc >= 2 else (f"cwqk{DC - 1}",), f"lwv{dc}")
            v_op(lambda e, dc=dc: e.tensor_copy(
                wv_b[:, dc * D:(dc + 1) * D], w_f[dc % 2][:, 0:D]),
                (f"lwv{dc}",), f"cwv{dc}")
        st_dma(lambda e: e.dma_start(
            out=bvrow[:], in_=ap["bv"].rearrange("(o t) -> o t", o=1)), (), "bvr",
            stream="SB")
        st_dma(lambda e: e.dma_start(
            out=b2r[:], in_=ap["b2v"].rearrange("(o t) -> o t", o=1)), (), "b2r",
            stream="SB")
        v_op(lambda e: e.memset(relrow[:, 32:33], 1.0), ("rel",), "rel1")
        t_op(lambda e: e.transpose(ps_c[0:128, 0:128], pstage[:], eye_t[:]),
             ("ppk", "eye"), "ptr")
        v_op(lambda e: e.tensor_copy(pcols[:], ps_c[0:128, 0:NPR]), ("ptr",), "pcl")
        v_op(lambda e: e.tensor_scalar(pc2[:, 0:1], pcols[:, 1:2], INV_S, None, ALU.mult),
             ("pcl",), "gqS")
        v_op(lambda e: e.tensor_scalar(pc2[:, 1:2], pcols[:, 2:3], INV_S, None, ALU.mult),
             ("pcl",), "bqS")
        v_op(lambda e: e.tensor_copy(pc2[:, 2:3], pcols[:, 3:4]), ("pcl",), "gk")
        v_op(lambda e: e.tensor_copy(pc2[:, 3:4], pcols[:, 4:5]), ("pcl",), "bk")
        v_op(lambda e: e.tensor_scalar(relrow[:, 0:32], relrow[:, 0:32], SQRT_DQK, None,
                                       ALU.mult), ("rel",), "rsc")
        t_op(lambda e: e.matmul(ps_d[0:128, 0:33], onesc[0:1, 0:128], relrow[:],
                                start=True, stop=True), ("rsc", "rel1", "ones"), "t33m")
        v_op(lambda e: e.tensor_copy(t33[:], ps_d[0:128, 0:33]), ("t33m",), "t33c")

        if NCT >= 48:
            CBS = [0, 8, 28, 48, 68, NCT]
        else:
            CBS = [0, NCT]

        def chunk_of(t):
            for ci in range(len(CBS) - 1):
                if t < CBS[ci + 1]:
                    return ci
            return len(CBS) - 2

        mq = []
        for half, dst in ((0, masters[:, 0:128]), (1, masters[:, 128:256])):
            for b in range(33):
                pi = half * 33 + b
                pp = pi % 6
                wl = [f"mac{pi - 6}"] if pi >= 6 else ()

                def m_thunk(pi=pi, pp=pp, wl=wl, b=b, half=half, dst=dst):
                    ld_dma(lambda e: e.dma_start(
                        out=plane_t[:, pp * 128:(pp + 1) * 128],
                        in_=ap["planes"][:, pi * 128:(pi + 1) * 128]), wl, f"lpl{pi}")
                    if b == 0:
                        v_op(lambda e: e.tensor_scalar(
                            dst, plane_t[:, pp * 128:(pp + 1) * 128], t33[:, 0:1],
                            None, ALU.mult), (f"lpl{pi}", "t33c"), f"mac{pi}")
                    elif b == 32 and half == 0:
                        v_op(lambda e: e.scalar_tensor_tensor(
                            dst, plane_t[:, pp * 128:(pp + 1) * 128], MASKVAL, dst,
                            ALU.mult, ALU.add), (f"lpl{pi}",), f"mac{pi}")
                    else:
                        coef = t33[:, b:b + 1] if b < 32 else pc2[:, 11:12]
                        v_op(lambda e: e.scalar_tensor_tensor(
                            dst, plane_t[:, pp * 128:(pp + 1) * 128], coef, dst,
                            ALU.mult, ALU.add), (f"lpl{pi}",), f"mac{pi}")
                mq.append(m_thunk)

        def m_pre():
            v_op(lambda e: e.tensor_scalar(masters[:, 256:384], onesc[:, 0:128],
                                           t33[:, 31:32], None, ALU.mult),
                 ("t33c",), "mfar")
            v_op(lambda e: e.memset(masters[:, 384:512], MASKVAL), (), "mmask")
        mq.append(m_pre)
        for t in range(16):
            def b_thunk(t=t):
                for mi in range(4):
                    cc = t * 4 + mi
                    if mi == 0:
                        v_op(lambda e, t=t, cc=cc: e.tensor_scalar(
                            bnd[:, t * 128:(t + 1) * 128], masters[:, 0:128],
                            farw[:, cc:cc + 1], None, ALU.mult),
                            ("ohb", "mac65", "mfar", "mmask") if t == 0 else (), None)
                    else:
                        v_op(lambda e, t=t, mi=mi, cc=cc: e.scalar_tensor_tensor(
                            bnd[:, t * 128:(t + 1) * 128],
                            masters[:, mi * 128:(mi + 1) * 128],
                            farw[:, cc:cc + 1], bnd[:, t * 128:(t + 1) * 128],
                            ALU.mult, ALU.add), (), f"bnd{t}" if mi == 3 else None)
            mq.append(b_thunk)

        def m_farw():
            v_op(lambda e: e.tensor_scalar(farw[:], onesc[:], t33[:, 31:32], None,
                                           ALU.mult), ("bnd15",), "farw")
        mq.append(m_farw)
        MQN = len(mq)
        mq_i = [0]

        def mdrip(target):
            while mq_i[0] < min(target, MQN):
                mq[mq_i[0]]()
                mq_i[0] += 1

        def emit_stats(t, c):
            p2 = t % 2
            w = [f"sq1_{t - 2}", f"rd1_{t - 2}"] if t >= 2 else ()
            ld_dma(lambda e, t=t, p2=p2: e.dma_start(
                out=evac[:, p2 * D:(p2 + 1) * D],
                in_=ap["xcov"][t * 128:(t + 1) * 128, :]), w, f"ldx{t}", stream="LX")
            s_op(lambda e, t=t, p2=p2: e.activation(
                sq_scr[:], evac[:, p2 * D:(p2 + 1) * D], AF.Square,
                accum_out=s1q[:, t:t + 1]), (f"ldx{t}",), f"sq1_{t}")
            v_op(lambda e, t=t, p2=p2: e.tensor_reduce(
                s1x[:, t:t + 1], evac[:, p2 * D:(p2 + 1) * D], AX, ALU.add),
                (f"ldx{t}",), f"rd1_{t}")

        def emit_rstd_chunk(c):
            a, b = CBS[c], CBS[c + 1]
            wlast = (f"sq1_{b - 1}", f"rd1_{b - 1}")
            v_op(lambda e: e.tensor_scalar(n1[:, a:b], s1x[:, a:b], 1.0 / D, None,
                                           ALU.mult), wlast, f"m1c{c}")
            v_op(lambda e: e.tensor_scalar(s1q[:, a:b], s1q[:, a:b], 1.0 / D, None,
                                           ALU.mult), wlast, f"e21c{c}")
            v_op(lambda e: e.tensor_tensor(s1x[:, a:b], n1[:, a:b], n1[:, a:b],
                                           ALU.mult), (f"m1c{c}",), f"ms1c{c}")
            v_op(lambda e: e.tensor_tensor(s1q[:, a:b], s1q[:, a:b], s1x[:, a:b],
                                           ALU.subtract), (f"ms1c{c}", f"e21c{c}"),
                 f"var1c{c}")
            s_op(lambda e: e.activation(r1[:, a:b], s1q[:, a:b], AF.Sqrt,
                                        bias=pc2[:, 4:5], scale=pc2[:, 11:12]),
                 (f"var1c{c}", "eps", "one1"), f"sqrt1c{c}")
            v_op(lambda e: e.reciprocal(r1[:, a:b], r1[:, a:b]),
                 (f"sqrt1c{c}",), f"rc1c{c}")
            v_op(lambda e: e.scalar_tensor_tensor(n1[:, a:b], n1[:, a:b], -1.0,
                                                  r1[:, a:b], ALU.mult, ALU.mult),
                 (f"rc1c{c}",), f"nm1c{c}")

        def w1p1_chunk(ch):
            wp = ch % 2
            w = [f"w1a{ch - 2}"] if ch >= 2 else (f"cwv{DC - 1}",)
            gd_dma(lambda e, ch=ch, wp=wp: e.dma_start(
                out=w_f[wp][:, 0:D], in_=ap["W1"][ch * 128:(ch + 1) * 128, :]),
                w, f"w1l{ch}", stream="LW")
            s_op(lambda e, ch=ch, wp=wp: e.activation(
                sq_scr[:, 0:D], w_f[wp][:, 0:D], AF.Abs,
                accum_out=wabs[:, ch:ch + 1]), (f"w1l{ch}",), f"w1a{ch}")

        W1N = {"i": 0}

        def emit_norm(t, cidx):
            p2 = t % 2
            w = [f"nm1c{cidx}"]
            if t >= 2:
                w.append(f"nrm{t - 2}")
            gd_dma(lambda e, t=t, p2=p2: e.dma_start(
                out=x_in[:, p2 * D:(p2 + 1) * D],
                in_=ap["xcov"][t * 128:(t + 1) * 128, :]), w, f"ldn{t}", stream="LN")
            w2 = [f"ldn{t}"]
            if t >= 4:
                w2.append(f"txe{t - 4}")
            s_op(lambda e, t=t: e.activation(
                xn_b[:, (t % 4) * D:(t % 4 + 1) * D], x_in[:, p2 * D:(p2 + 1) * D],
                AF.Identity, bias=n1[:, t:t + 1], scale=r1[:, t:t + 1]), w2, f"nrm{t}")
            while W1N["i"] < W1CH and W1N["i"] < t + 1:
                w1p1_chunk(W1N["i"])
                W1N["i"] += 1
            if t >= 8:
                mdrip(4 * (t - 7))

        ps_tr = [ps_c.bitcast(BF16), ps_d.bitcast(BF16)]

        def emit_proj(gq):
            gp = gq % 2
            for tt in range(4):
                t = gq * 4 + tt
                tb = t % 2
                wtr = [f"nrm{t}", "eyb"]
                if t >= 2:
                    wtr.append(f"txe{t - 2}")
                for dc in range(DC):
                    t_op(lambda e, t=t, tb=tb, dc=dc: e.transpose(
                        ps_tr[tb][:, dc * 128:(dc + 1) * 128],
                        xn_b[:, (t % 4) * D + dc * 128:(t % 4) * D + (dc + 1) * 128],
                        eye_b[:]),
                        wtr if dc == 0 else (), f"txp{t}" if dc == DC - 1 else None)
                wev = [f"txp{t}"]
                if gq >= 2:
                    wev.append(f"pj{gq - 2}")
                for dc in range(DC):
                    v_op(lambda e, t=t, tb=tb, gp=gp, dc=dc, tt=tt: e.tensor_copy(
                        xnT[:, gp * DC * 512 + dc * 512 + tt * 128:
                            gp * DC * 512 + dc * 512 + tt * 128 + 128],
                        ps_tr[tb][:, dc * 128:(dc + 1) * 128]),
                        wev if dc == 0 else (), f"txe{t}" if dc == DC - 1 else None)
            t = gq * 4 + 3
            wtd = [f"txe{t}", f"txe{t - 1}", f"txe{t - 2}", f"txe{t - 3}"]
            if gq < NGC:
                for tt in range(4):
                    j = gq * 4 + tt
                    po = j % 2
                    wv_w = list(wtd) + [f"cwv{DC - 1}", "bvr", "onesbf"]
                    if j >= 2:
                        wv_w.append(f"vs{j - 2}")
                    for dc in range(DC):
                        for half in range(HD):
                            t_op(lambda e, po=po, gp=gp, dc=dc, half=half, tt=tt: e.matmul(
                                ps_o[po][:, half * 512:(half + 1) * 512],
                                xnT[:, gp * DC * 512 + dc * 512 + tt * 128:
                                    gp * DC * 512 + dc * 512 + tt * 128 + 128],
                                wv_b[:, dc * D + half * 512:dc * D + (half + 1) * 512],
                                start=(dc == 0), stop=False),
                                wv_w if (half == 0 and dc == 0) else ())
                    for half in range(HD):
                        t_op(lambda e, po=po, half=half: e.matmul(
                            ps_o[po][:, half * 512:(half + 1) * 512], ones_bfr[:],
                            bvrow[0:1, half * 512:(half + 1) * 512],
                            start=False, stop=True),
                            (), f"tv{j}" if half == HD - 1 else None)
                    if j < HBASE:
                        s_op(lambda e, j=j, po=po: e.activation(
                            vw[:, j * D:(j + 1) * D], ps_o[po][:, 0:D], AF.Silu),
                            (f"tv{j}",), f"vs{j}")
                    else:
                        w4 = [f"tv{j}"]
                        if j >= HBASE + 2:
                            w4.append(f"svh{j - 2}")
                        s_op(lambda e, j=j, po=po: e.activation(
                            vstg[:, po * D:(po + 1) * D], ps_o[po][:, 0:D], AF.Silu),
                            w4, f"vs{j}")
                        td_dma(lambda e, j=j, po=po: e.dma_start(
                            out=vhi_d[:, (j - HBASE) * D:(j - HBASE + 1) * D],
                            in_=vstg[:, po * D:(po + 1) * D]), (f"vs{j}",),
                            f"svh{j}", stream="SV")
                wq = list(wtd) + [f"cwqk{DC - 1}", "pcl"]
                if gq >= 2:
                    wq.append(f"kaf{gq - 2}")
                for dc in range(DC):
                    t_op(lambda e, gq=gq, gp=gp, dc=dc: e.matmul(
                        ps_ab[gp][:, 0:512], wqk_b[:, dc * 128:(dc + 1) * 128],
                        xnT[:, gp * DC * 512 + dc * 512:gp * DC * 512 + (dc + 1) * 512],
                        start=(dc == 0), stop=(dc == DC - 1)),
                        wq if dc == 0 else (), f"tq{gq}" if dc == DC - 1 else None)
                s_op(lambda e, gq=gq, gp=gp: e.activation(
                    qks[:, gp * 512:(gp + 1) * 512], ps_ab[gp][:], AF.Silu,
                    bias=pcols[:, 0:1]), (f"tq{gq}",), f"qs{gq}")
                s_op(lambda e, gq=gq, gp=gp: e.activation(
                    kT[:, gq * 512:(gq + 1) * 512], qks[:, gp * 512:(gp + 1) * 512],
                    AF.Identity, bias=pc2[:, 3:4], scale=pc2[:, 2:3]),
                    (f"qs{gq}", "gk", "bk"), f"kaf{gq}")
            else:
                wq = list(wtd) + [f"cwqk{DC - 1}", "pcl"]
                if gq >= 2:
                    wq.append(f"qs{gq - 2}")
                for dc in range(DC):
                    t_op(lambda e, gq=gq, gp=gp, dc=dc: e.matmul(
                        ps_ab[gp][:, 0:512], wqk_b[:, dc * 128:(dc + 1) * 128],
                        xnT[:, gp * DC * 512 + dc * 512:gp * DC * 512 + (dc + 1) * 512],
                        start=(dc == 0), stop=(dc == DC - 1)),
                        wq if dc == 0 else (), f"tq{gq}" if dc == DC - 1 else None)
                s_op(lambda e, gq=gq, gp=gp: e.activation(
                    qks[:, gp * 512:(gp + 1) * 512], ps_ab[gp][:], AF.Silu,
                    bias=pcols[:, 0:1]), (f"tq{gq}",), f"qs{gq}")
                oq = gq - NGC
                s_op(lambda e, gq=gq, gp=gp, oq=oq: e.activation(
                    qT[:, oq * 512:(oq + 1) * 512], qks[:, gp * 512:(gp + 1) * 512],
                    AF.Identity, bias=pc2[:, 1:2], scale=pc2[:, 0:1]),
                    (f"qs{gq}", "gqS", "bqS"), f"qaf{gq}")
            op("tensor", lambda e: e.nop(), (f"tq{gq}",), f"pj{gq}", "T")

        NGALL = NCT // 4
        sc = CBS[1]
        for t in range(CBS[1]):
            emit_stats(t, 0)
        emit_rstd_chunk(0)
        rstd_done = 1
        ncur = [0]

        def flush_norms(limit):
            while ncur[0] < min(limit, NCT) and chunk_of(ncur[0]) < rstd_done:
                emit_norm(ncur[0], chunk_of(ncur[0]))
                ncur[0] += 1

        flush_norms(4)
        for gq in range(NGALL):
            target = min(NCT, (gq + 1) * 4 + (CBS[2] - CBS[1] if len(CBS) > 2 else NCT))
            while sc < target:
                emit_stats(sc, chunk_of(sc))
                sc += 1
                if rstd_done < len(CBS) - 1 and sc == CBS[rstd_done + 1]:
                    emit_rstd_chunk(rstd_done)
                    rstd_done += 1
            flush_norms((gq + 2) * 4)
            emit_proj(gq)
        mdrip(MQN)
        while W1N["i"] < W1CH:
            w1p1_chunk(W1N["i"])
            W1N["i"] += 1

        v_op(lambda e: e.tensor_reduce(wred[:, 0:1], wabs[:, 0:W1CH], AX, ALU.add),
             (f"w1a{W1CH - 1}",), "w1r")
        t_op(lambda e: e.matmul(ps_d[0:1, 0:1], ones_fc[:], wred[:, 0:1],
                                start=True, stop=True), ("w1r", "onesfc"), "w1t")
        v_op(lambda e: e.tensor_scalar(cscal[0:1, 0:1], ps_d[0:1, 0:1],
                                       0.5 / (FF * D), 5e-9, ALU.mult, ALU.max),
             ("w1t",), "w1c")
        t_op(lambda e: e.matmul(ps_d[0:128, 1:2], onesc[0:1, 0:128], cscal[0:1, 0:1],
                                start=True, stop=True), ("w1c",), "w1cb")
        v_op(lambda e: e.tensor_copy(pc2[:, 7:8], ps_d[0:128, 1:2]), ("w1cb",), "c1")
        v_op(lambda e: e.tensor_scalar(pc2[:, 8:9], pc2[:, 7:8], -1.0, None, ALU.mult),
             ("c1",), "nc1")

        def w2p1_chunk(ch):
            wp = ch % 4
            if ch >= 4:
                w = [f"w2a{ch - 4}"]
            elif ch < 2:
                w = (f"w1a{W1CH - 1}",)
            else:
                w = ("farw",)
            gd_dma(lambda e, ch=ch, wp=wp: e.dma_start(
                out=w_f4[wp][:], in_=ap["W2"][
                    (ch // W2CQ) * 128:(ch // W2CQ + 1) * 128,
                    (ch % W2CQ) * 1024:(ch % W2CQ + 1) * 1024]), w, f"w2l{ch}", stream="LW")
            s_op(lambda e, ch=ch, wp=wp: e.activation(
                sq_scr[:], w_f4[wp][:, 0:1024], AF.Abs,
                accum_out=wabs[:, 32 + ch:33 + ch]), (f"w2l{ch}",), f"w2a{ch}")

        def c2_ops():
            v_op(lambda e: e.tensor_reduce(wred[:, 1:2], wabs[:, 32:32 + W2CH], AX,
                                           ALU.add), (f"w2a{W2CH - 1}",), "w2r")
            t_op(lambda e: e.matmul(ps_d[0:1, 1:2], ones_fc[:], wred[:, 1:2],
                                    start=True, stop=True), ("w2r",), "w2t")
            v_op(lambda e: e.tensor_scalar(cscal[0:1, 1:2], ps_d[0:1, 1:2],
                                           0.5 / (FF * D), 5e-9, ALU.mult, ALU.max),
                 ("w2t",), "w2c")
            t_op(lambda e: e.matmul(ps_d[0:128, 2:3], onesc[0:1, 0:128],
                                    cscal[0:1, 1:2], start=True, stop=True),
                 ("w2c",), "w2cb")
            v_op(lambda e: e.tensor_copy(pc2[:, 9:10], ps_d[0:128, 2:3]), ("w2cb",), "c2")
            v_op(lambda e: e.tensor_scalar(pc2[:, 10:11], pc2[:, 9:10], -1.0, None,
                                           ALU.mult), ("c2",), "nc2")

        def wp2_chunk(pref, ch, srcfn, dstfn, cpos, ncpos, firstw, cw):
            wp = ch % 4
            wp2 = ch % 2
            w = [f"{pref}v{ch - 4}"] if ch >= 4 else firstw
            gd_dma(lambda e, ch=ch, wp=wp: e.dma_start(
                out=w_f4[wp][:, 0:cw], in_=srcfn(ch)), w, f"{pref}m{ch}", stream="LW")
            wg = [f"{pref}m{ch}"]
            if ch >= 2:
                wg.append(f"{pref}v{ch - 2}")
            v_op(lambda e, ch=ch, wp=wp, wp2=wp2: e.tensor_scalar(
                w_le[wp2][:, 0:cw], w_f4[wp][:, 0:cw], pc2[:, ncpos:ncpos + 1], None,
                ALU.is_le), wg, f"{pref}g{ch}")
            wv_ = [f"{pref}g{ch}"]
            if ch >= 2:
                wv_.append(f"{pref}s{ch - 2}")
            v_op(lambda e, ch=ch, wp=wp, wp2=wp2: e.scalar_tensor_tensor(
                w_tb[wp2][:, 0:cw], w_f4[wp][:, 0:cw], pc2[:, cpos:cpos + 1],
                w_le[wp2][:, 0:cw], ALU.is_ge, ALU.subtract), wv_, f"{pref}v{ch}")
            st_dma(lambda e, ch=ch, wp2=wp2: e.dma_start(
                out=dstfn(ch), in_=w_tb[wp2][:, 0:cw]),
                (f"{pref}v{ch}",), f"{pref}s{ch}", stream="SW")

        w1src = lambda ch: ap["W1"][ch * 128:(ch + 1) * 128, :]
        w1dst = lambda ch: w1t_d[ch * 128:(ch + 1) * 128, :]
        w2src = lambda ch: ap["W2"][(ch // W2CQ) * 128:(ch // W2CQ + 1) * 128,
                                    (ch % W2CQ) * 1024:(ch % W2CQ + 1) * 1024]
        w2dst = lambda ch: w2t_d[(ch // W2CQ) * 128:(ch // W2CQ + 1) * 128,
                                 (ch % W2CQ) * 1024:(ch % W2CQ + 1) * 1024]

        def bias_ap(q, E, g):
            G = E // 4
            if g < G - 2:
                return farw[:]
            isA = q >= M
            pos = 4 * g - (E - 8)
            base = 0 if isA else 8
            return bnd[:, (base + pos) * 128:(base + pos + 4) * 128]

        w2p1_i = 0
        w1p2_i = 0
        w2p2_i = 0
        ps_sim = [ps_a, ps_b, ps_c]
        HIGH_LAST_USER = {}
        LOW_LAST = {}
        att_st = xr_buf.bitcast(BF16)
        gx = 0
        GH = NT // 8

        chunkq = []
        for ch in range(W2CH):
            chunkq.append(lambda ch=ch: w2p1_chunk(ch))
            if ch == W2CH - 1:
                chunkq.append(c2_ops)
        for ch in range(W1CH):
            chunkq.append(lambda ch=ch: wp2_chunk(
                "w1", ch, w1src, w1dst, 7, 8, ("nc1", f"w2a{W2CH - 1}"), D))
        for ch in range(W2CH):
            chunkq.append(lambda ch=ch: wp2_chunk(
                "w2", ch, w2src, w2dst, 9, 10, ("nc2", f"w1v{W1CH - 1}"), 1024))
        NCHQ = len(chunkq)
        TOTU = sum(e // 4 for e in exts0)
        cq_i = [0]

        def drip(target):
            while cq_i[0] < min(target, NCHQ):
                chunkq[cq_i[0]]()
                cq_i[0] += 1

        for p in range(OWN // 2):
            qa, qb = 2 * p, 2 * p + 1
            gx0 = gx
            units = []
            for g in range(max(exts0[qa], exts0[qb]) // 4):
                for q in (qa, qb):
                    if g < exts0[q] // 4:
                        units.append((g, q))
                        if g < GH:
                            LOW_LAST[g] = gx + len(units) - 1
            U = len(units)
            firstu, lastu, glast = {}, {}, {}
            for ui, (g, q) in enumerate(units):
                firstu.setdefault(q, ui)
                lastu[q] = ui
                glast[g] = ui
            for g in range(GH, max(glast) + 1):
                hw = [f"svh{4 * g + 3}", f"svh{4 * g + 2}"]
                lu = HIGH_LAST_USER.get(g % 2)
                if lu is not None:
                    hw.append(f"tag{lu}")
                elif p == 0:
                    hw.append(f"vs{NT - 1}")
                HIGH_LAST_USER[g % 2] = gx0 + glast[g]
                j0 = 4 * g - HBASE
                ld_dma(lambda e, g=g, j0=j0: e.dma_start(
                    out=wvh[:, (g % 2) * 4 * D:(g % 2 + 1) * 4 * D],
                    in_=vhi_d[:, j0 * D:(j0 + 4) * D]),
                    hw, f"vh{p}_{g}", stream="LV")
            for ui in range(U + 1):
                if ui < U:
                    g, q = units[ui]
                    gi = gx0 + ui
                    gb = gi % 2
                    gp4 = gi % 3
                    wsim = []
                    if p == 0 and ui == 0:
                        wsim += [f"kaf{NGC - 1}", f"qaf{NG - 1}", "farw", "nc1"]
                    if gi >= 3:
                        wsim.append(f"vbg{gi - 3}")
                    for r in range(4):
                        t_op(lambda e, g=g, gp4=gp4, r=r, q=q: e.matmul(
                            ps_sim[gp4][:, r * 128:(r + 1) * 128],
                            kT[:, (4 * g + r) * 128:(4 * g + r + 1) * 128],
                            qT[:, q * 128:(q + 1) * 128], start=True, stop=True),
                            wsim if r == 0 else (), f"tsg{gi}" if r == 3 else None)
                if ui >= 1:
                    gg, qq = units[ui - 1]
                    gj = gx0 + ui - 1
                    ggb = gj % 2
                    po = qq % 2
                    wav = [f"vag{gj}"]
                    if gg >= GH:
                        wav.append(f"vh{p}_{gg}")
                    first = (ui - 1 == firstu[qq])
                    islast = (ui - 1 == lastu[qq])
                    if first:
                        if qq >= 2:
                            wav += [f"se{qq - 2}", f"vr{qq - 2}"]
                        else:
                            wav.append(f"vs{NT - 1}")
                    for r in range(4):
                        if gg < GH:
                            vsl = lambda r, half, gg=gg: vw[
                                :, (4 * gg + r) * D + half * 512:
                                (4 * gg + r) * D + (half + 1) * 512]
                        else:
                            vsl = lambda r, half, gg=gg: wvh[
                                :, (gg % 2) * 4 * D + r * D + half * 512:
                                (gg % 2) * 4 * D + r * D + (half + 1) * 512]
                        for half in range(HD):
                            st_ = first and r == 0
                            sp_ = (r == 3) and (half == HD - 1)
                            sp2_ = (r == 3) and islast
                            t_op(lambda e, ggb=ggb, r=r, half=half, po=po,
                                 st_=st_, sp2_=sp2_, vsl=vsl: e.matmul(
                                     ps_o[po][:, half * 512:(half + 1) * 512],
                                     attn2[ggb][:, r * 128:(r + 1) * 128],
                                     vsl(r, half),
                                     start=st_, stop=sp2_),
                                 wav if (r == 0 and half == 0) else (),
                                 f"tag{gj}" if sp_ else None)
                if ui < U:
                    wb = [f"tsg{gi}"]
                    if gi >= 2:
                        wb.append(f"seg{gi - 2}")
                    bs = bias_ap(q, exts0[q], g)
                    v_op(lambda e, gp4=gp4, gb=gb, bs=bs: e.tensor_tensor(
                        simb[gb][:], ps_sim[gp4][:], bs, ALU.add), wb, f"vbg{gi}")
                    we = [f"vbg{gi}"]
                    if gi >= 2:
                        we.append(f"vag{gi - 2}")
                    s_op(lambda e, gb=gb: e.activation(
                        attn_b[gb][:], simb[gb][:], AF.Erf, bias=pc2[:, 6:7],
                        scale=pc2[:, 5:6]), we, f"seg{gi}")
                    wa = [f"seg{gi}"]
                    if gi >= 2:
                        wa.append(f"tag{gi - 2}")
                    v_op(lambda e, gb=gb: e.tensor_scalar(
                        attn2[gb][:], attn_b[gb][:], 0.5, 0.5, ALU.mult, ALU.add),
                        wa, f"vag{gi}")
                    drip(((gi + 1) * NCHQ) // TOTU + 1)
            gx = gx0 + U
            for q in (qa, qb):
                po = q % 2
                gl = gx0 + lastu[q]
                s_op(lambda e, q=q, po=po: e.activation(
                    sq_scr[:], ps_o[po][:, 0:D], AF.Square,
                    accum_out=st2[:, OWN + q:OWN + q + 1]), (f"tag{gl}",), f"sr{q}")
                wev = [f"sr{q}"]
                if q >= 2:
                    wev.append(f"sat{q - 2}")
                s_op(lambda e, q=q, po=po: e.activation(
                    att_st[:, po * 2 * D:po * 2 * D + D], ps_o[po][:, 0:D],
                    AF.Identity, accum_out=st2[:, q:q + 1]), wev, f"se{q}")
                v_op(lambda e: e.nop(), (f"se{q}",), f"vr{q}")
                td_dma(lambda e, q=q, po=po: e.dma_start(
                    out=att_d[q * 128:(q + 1) * 128, :],
                    in_=att_st[:, po * 2 * D:po * 2 * D + D]),
                    (f"se{q}",), f"sat{q}", stream="SA")
        drip(NCHQ)

        lastq = OWN - 1
        GXT = sum(e // 4 for e in exts0)
        for dc in range(DC):
            wv_ = [f"w1s{W1CH - 1}", f"w1s{W1CH - 2}"] if dc == 0 else []
            if dc in LOW_LAST:
                wv_.append(f"tag{LOW_LAST[dc]}")
            td_dma(lambda e, dc=dc: e.dma_start_transpose(
                vw[:, dc * FF:(dc + 1) * FF], w1t_d[:, dc * 128:(dc + 1) * 128]),
                wv_, f"tw1_{dc}", stream="TW")

        def rstd_batch(stt, wlist, tag):
            v_op(lambda e, stt=stt: e.tensor_scalar(
                stt[:, 3 * OWN:4 * OWN], stt[:, 0:OWN], 1.0 / D, None, ALU.mult),
                wlist, f"{tag}m")
            v_op(lambda e, stt=stt: e.tensor_scalar(
                stt[:, OWN:2 * OWN], stt[:, OWN:2 * OWN], 1.0 / D, None, ALU.mult),
                wlist, f"{tag}e")
            v_op(lambda e, stt=stt: e.tensor_tensor(
                stt[:, 0:OWN], stt[:, 3 * OWN:4 * OWN], stt[:, 3 * OWN:4 * OWN],
                ALU.mult), (f"{tag}m",), f"{tag}s")
            v_op(lambda e, stt=stt: e.tensor_tensor(
                stt[:, OWN:2 * OWN], stt[:, OWN:2 * OWN], stt[:, 0:OWN], ALU.subtract),
                (f"{tag}s", f"{tag}e"), f"{tag}v")
            s_op(lambda e, stt=stt: e.activation(
                stt[:, 2 * OWN:3 * OWN], stt[:, OWN:2 * OWN], AF.Sqrt,
                bias=pc2[:, 4:5], scale=pc2[:, 11:12]), (f"{tag}v",), f"{tag}q")
            v_op(lambda e, stt=stt: e.reciprocal(
                stt[:, 2 * OWN:3 * OWN], stt[:, 2 * OWN:3 * OWN]), (f"{tag}q",), f"{tag}r")
            v_op(lambda e, stt=stt: e.scalar_tensor_tensor(
                stt[:, 3 * OWN:4 * OWN], stt[:, 3 * OWN:4 * OWN], -1.0,
                stt[:, 2 * OWN:3 * OWN], ALU.mult, ALU.mult), (f"{tag}r",), f"{tag}n")

        rstd_batch(st2, (f"sr{OWN - 1}", f"vr{OWN - 1}"), "r2")

        NG3 = OWN // 4 if OWN >= 4 else 1
        TPG = min(4, OWN)
        psy = [(ps_o0[:, 0:512], ps_o0[:, 512:1024]),
               (ps_o1[:, 0:512], ps_o1[:, 512:1024]),
               (ps_a[:, 0:512], ps_b[:, 0:512]),
               (ps_c[:, 0:512], ps_d[:, 0:512])]
        lastq = OWN - 1
        GXT2 = sum(e // 4 for e in exts0)

        def emit_ffn1(g3):
            gp = (g3 + 1) % 2
            for fc in range(FC):
                fp = fc % 2
                wf = [f"t3_{g3}_{DC - 1}", f"t3_{g3}_{DC - 2}"]
                if fc == 0 and g3 == 0:
                    wf += [f"tw1_{DC - 1}", f"tw1_{DC - 2}", f"tw1_{DC - 3}",
                           f"tw1_{DC - 4}", f"vbg{GXT2 - 1}",
                           f"w2s{W2CH - 1}", f"w2s{W2CH - 2}", "nc2"]
                if fc >= 2:
                    wf.append(f"gl{g3}_{fc - 2}")
                elif g3 >= 1:
                    wf.append(f"gl{g3 - 1}_{FC - 1}")
                if g3 >= 1 and fc < 2:
                    wf += [f"sey_{g3 * TPG - 1}", f"v4b{g3 * TPG - 1}"]
                for dc in range(DC):
                    t_op(lambda e, gp=gp, fc=fc, fp=fp, dc=dc: e.matmul(
                        ps_cd[fp][:],
                        vw[:, dc * FF + fc * 128:dc * FF + (fc + 1) * 128],
                        xnT[:, gp * DC * 512 + dc * 512:gp * DC * 512 + (dc + 1) * 512],
                        start=(dc == 0), stop=(dc == DC - 1)),
                        wf if dc == 0 else (), f"f1_{g3}_{fc}" if dc == DC - 1 else None)
                wg = [f"f1_{g3}_{fc}"]
                if g3 >= 1 and fc < 2:
                    wg.append(f"t2_{g3 - 1}_{FC - 1}")
                if g3 == 0 and fc < 2:
                    wg += [f"w2s{W2CH - 1}", f"w2s{W2CH - 2}"]
                s_op(lambda e, fc=fc, fp=fp: e.activation(
                    hT[:, fc * 512:(fc + 1) * 512], ps_cd[fp][:], AF.Gelu,
                    bias=pcols[:, 5 + fc:5 + fc + 1]), wg, f"gl{g3}_{fc}")

        def emit_ffn2(g3):
            for fc in range(FC):
                wl = []
                fi = g3 * FC + fc
                if fi >= 2:
                    wl.append(f"t2_{g3}_{fc - 2}" if fc >= 2
                              else f"t2_{g3 - 1}_{FC - (2 - fc)}")
                if fi == 0:
                    wl += [f"w2s{W2CH - 1}", f"w2s{W2CH - 2}"]
                td_dma(lambda e, fc=fc: e.dma_start_transpose(
                    w2w[:, (fc % 2) * D:(fc % 2 + 1) * D],
                    w2t_d[:, fc * 128:(fc + 1) * 128]), wl, f"t2l{g3}_{fc}",
                    stream="TW")
                for tt in range(TPG):
                    wm = [f"t2l{g3}_{fc}"]
                    if fc == 0:
                        wm.append(f"gl{g3}_{FC - 1}")
                        q4 = g3 * TPG + tt
                        if g3 == 0:
                            if tt < 2:
                                wm += [f"se{OWN - 2 + tt}", f"vr{OWN - 2 + tt}"]
                        else:
                            wm += [f"sey_{q4 - TPG}", f"v4b{q4 - TPG}"]
                    for half in range(2):
                        t_op(lambda e, g3=g3, fc=fc, tt=tt, half=half: e.matmul(
                            psy[tt][half],
                            hT[:, fc * 512 + tt * 128:fc * 512 + tt * 128 + 128],
                            w2w[:, (fc % 2) * D + half * 512:
                                (fc % 2) * D + (half + 1) * 512],
                            start=(fc == 0), stop=False),
                            wm if half == 0 else (),
                            f"t2_{g3}_{fc}" if (tt == TPG - 1 and half == 1) else None)
            for tt in range(TPG):
                q4 = g3 * TPG + tt
                for half in range(2):
                    t_op(lambda e, tt=tt, half=half: e.matmul(
                        psy[tt][half], ones_bfr[:],
                        b2r[0:1, half * 512:(half + 1) * 512], start=False, stop=True),
                        ("b2r",) if (tt == 0 and half == 0) else (),
                        f"tb2_{g3}_{tt}" if half == 1 else None)
                wsq0 = [f"tb2_{g3}_{tt}"]
                if q4 >= 1:
                    wsq0.append(f"v4b{q4 - 1}")
                s_op(lambda e, tt=tt: e.activation(
                    sq_scr[:, 0:512], psy[tt][0], AF.Square,
                    accum_out=st4[:, 4 * OWN:4 * OWN + 1]), wsq0, None)
                s_op(lambda e, tt=tt: e.activation(
                    sq_scr[:, 512:1024], psy[tt][1], AF.Square,
                    accum_out=st4[:, 4 * OWN + 1:4 * OWN + 2]), (), f"s4h{q4}")
                p2 = q4 % 2
                wev = [f"s4h{q4}"]
                if q4 >= 1:
                    wev.append(f"v4a{q4 - 1}")
                if q4 >= 2:
                    wev.append(f"sy{q4 - 2}")
                else:
                    wev += [f"svh{NT - 1}", f"svh{NT - 2}"]
                s_op(lambda e, tt=tt, p2=p2: e.activation(
                    vstg[:, p2 * D:p2 * D + 512], psy[tt][0], AF.Identity,
                    accum_out=st4[:, 4 * OWN + 2:4 * OWN + 3]), wev, None)
                s_op(lambda e, tt=tt, p2=p2: e.activation(
                    vstg[:, p2 * D + 512:(p2 + 1) * D], psy[tt][1], AF.Identity,
                    accum_out=st4[:, 4 * OWN + 3:4 * OWN + 4]), (), f"se4c{q4}")
                v_op(lambda e, q4=q4: e.tensor_tensor(
                    st4[:, q4:q4 + 1], st4[:, 4 * OWN + 2:4 * OWN + 3],
                    st4[:, 4 * OWN + 3:4 * OWN + 4], ALU.add),
                    (f"se4c{q4}",), f"v4a{q4}")
                v_op(lambda e, q4=q4: e.tensor_tensor(
                    st4[:, OWN + q4:OWN + q4 + 1], st4[:, 4 * OWN:4 * OWN + 1],
                    st4[:, 4 * OWN + 1:4 * OWN + 2], ALU.add),
                    (f"s4h{q4}",), f"v4b{q4}")
                s_op(lambda e: e.nop(), (f"v4b{q4}",), f"sey_{q4}")
                td_dma(lambda e, q4=q4, p2=p2: e.dma_start(
                    out=y_d[q4 * 128:(q4 + 1) * 128, :],
                    in_=vstg[:, p2 * D:(p2 + 1) * D]), (f"sey_{q4}",), f"sy{q4}",
                    stream="SY")

        for q in range(OWN):
            p2 = q % 2
            w = [f"sat{q}"]
            if q >= 2:
                w.append(f"n2_{q - 2}")
            else:
                w += [f"w2s{W2CH - 1}", f"w2s{W2CH - 2}"]
            ld_dma(lambda e, q=q, p2=p2: e.dma_start(
                out=x3bf[:, p2 * 2 * D:p2 * 2 * D + D],
                in_=att_d[q * 128:(q + 1) * 128, :]), w, f"lat{q}", stream="LA")
            w = []
            if q >= 2:
                w.append(f"n3_{q - 2}")
            else:
                w.append(f"sat{OWN - 2 + q}")
            ld_dma(lambda e, q=q, p2=p2: e.dma_start(
                out=xr_buf[:, p2 * D:(p2 + 1) * D],
                in_=ap["xcov"][(NT + q) * 128:(NT + q + 1) * 128, :]), w, f"lxo{q}", stream="LX")
            wn = [f"lat{q}", "r2n"]
            if q >= 2:
                wn.append(f"vx2_{q - 2}")
            else:
                wn.append(f"sat{OWN - 2 + q}")
            s_op(lambda e, q=q, p2=p2: e.activation(
                evac[:, p2 * D:(p2 + 1) * D], x3bf[:, p2 * 2 * D:p2 * 2 * D + D],
                AF.Identity, bias=st2[:, 3 * OWN + q:3 * OWN + q + 1],
                scale=st2[:, 2 * OWN + q:2 * OWN + q + 1]), wn, f"n2_{q}")
            v_op(lambda e, q=q, p2=p2: e.tensor_tensor(
                xr_buf[:, p2 * D:(p2 + 1) * D], evac[:, p2 * D:(p2 + 1) * D],
                xr_buf[:, p2 * D:(p2 + 1) * D], ALU.add),
                (f"n2_{q}", f"lxo{q}"), f"vx2_{q}")
            st_dma(lambda e, q=q, p2=p2: e.dma_start(
                out=x2_d[q * 128:(q + 1) * 128, :], in_=xr_buf[:, p2 * D:(p2 + 1) * D]),
                (f"vx2_{q}",), f"sx2_{q}", stream="SX")
            s_op(lambda e, q=q, p2=p2: e.activation(
                sq_scr[:], xr_buf[:, p2 * D:(p2 + 1) * D], AF.Square,
                accum_out=st3[:, OWN + q:OWN + q + 1]), (f"vx2_{q}",), f"s3q{q}")
            v_op(lambda e, q=q, p2=p2: e.tensor_reduce(
                st3[:, q:q + 1], xr_buf[:, p2 * D:(p2 + 1) * D], AX, ALU.add),
                (f"vx2_{q}",), f"v3x{q}")
            v_op(lambda e, q=q: e.tensor_scalar(
                st3[:, 3 * OWN + q:3 * OWN + q + 1], st3[:, q:q + 1], 1.0 / D, None,
                ALU.mult), (f"v3x{q}",), f"m3_{q}")
            v_op(lambda e, q=q: e.tensor_scalar(
                st3[:, OWN + q:OWN + q + 1], st3[:, OWN + q:OWN + q + 1], 1.0 / D, None,
                ALU.mult), (f"s3q{q}",), f"e3_{q}")
            v_op(lambda e, q=q: e.tensor_tensor(
                st3[:, q:q + 1], st3[:, 3 * OWN + q:3 * OWN + q + 1],
                st3[:, 3 * OWN + q:3 * OWN + q + 1], ALU.mult), (f"m3_{q}",), f"ms3_{q}")
            v_op(lambda e, q=q: e.tensor_tensor(
                st3[:, OWN + q:OWN + q + 1], st3[:, OWN + q:OWN + q + 1],
                st3[:, q:q + 1], ALU.subtract), (f"ms3_{q}", f"e3_{q}"), f"v3_{q}")
            s_op(lambda e, q=q: e.activation(
                st3[:, 2 * OWN + q:2 * OWN + q + 1], st3[:, OWN + q:OWN + q + 1],
                AF.Sqrt, bias=pc2[:, 4:5], scale=pc2[:, 11:12]), (f"v3_{q}",), f"q3_{q}")
            v_op(lambda e, q=q: e.reciprocal(
                st3[:, 2 * OWN + q:2 * OWN + q + 1],
                st3[:, 2 * OWN + q:2 * OWN + q + 1]), (f"q3_{q}",), f"r3_{q}")
            v_op(lambda e, q=q: e.scalar_tensor_tensor(
                st3[:, 3 * OWN + q:3 * OWN + q + 1],
                st3[:, 3 * OWN + q:3 * OWN + q + 1], -1.0,
                st3[:, 2 * OWN + q:2 * OWN + q + 1], ALU.mult, ALU.mult),
                (f"r3_{q}",), f"n3c_{q}")
            w4 = [f"n3c_{q}"]
            if q >= 2:
                w4.append(f"sn3_{q - 2}")
            s_op(lambda e, q=q, p2=p2: e.activation(
                evbf[:, p2 * 2 * D + D:p2 * 2 * D + 2 * D],
                xr_buf[:, p2 * D:(p2 + 1) * D], AF.Identity,
                bias=st3[:, 3 * OWN + q:3 * OWN + q + 1],
                scale=st3[:, 2 * OWN + q:2 * OWN + q + 1]), w4, f"n3_{q}")
            td_dma(lambda e, q=q, p2=p2: e.dma_start(
                out=xn3_d[q * 128:(q + 1) * 128, :],
                in_=evbf[:, p2 * 2 * D + D:p2 * 2 * D + 2 * D]), (f"n3_{q}",),
                f"sn3_{q}", stream="SN")
            if (q + 1) % TPG == 0:
                g3 = q // TPG
                gp = (g3 + 1) % 2
                for dc in range(DC):
                    w5 = [f"sn3_{q}", f"sn3_{q - 1}"]
                    if g3 == 1:
                        w5.append(f"tag{TOTU - 1}")
                    if g3 >= 2:
                        w5.append(f"f1_{g3 - 2}_{FC - 1}")
                    if g3 == 0:
                        ld_dma(lambda e, g3=g3, gp=gp, dc=dc: e.dma_start_transpose(
                            xnT[:, gp * DC * 512 + dc * 512:
                                gp * DC * 512 + (dc + 1) * 512],
                            xn3_d[g3 * TPG * 128:(g3 + 1) * TPG * 128,
                                  dc * 128:(dc + 1) * 128]),
                            w5, f"t3_{g3}_{dc}", stream="T3")
                    else:
                        td_dma(lambda e, g3=g3, gp=gp, dc=dc: e.dma_start_transpose(
                            xnT[:, gp * DC * 512 + dc * 512:
                                gp * DC * 512 + (dc + 1) * 512],
                            xn3_d[g3 * TPG * 128:(g3 + 1) * TPG * 128,
                                  dc * 128:(dc + 1) * 128]),
                            w5, f"t3_{g3}_{dc}", stream="T5")
                if g3 >= 1:
                    emit_ffn2(g3 - 1)
                emit_ffn1(g3)
        emit_ffn2(NG3 - 1)

        rstd_batch(st4, (f"s4h{OWN - 1}", f"v4b{OWN - 1}"), "r4")
        for q in range(OWN):
            p2 = q % 2
            w = ["r4n", f"sy{q}"]
            if q >= 2:
                w.append(f"so{q - 2}")
            ld_dma(lambda e, q=q, p2=p2: e.dma_start(
                out=x3bf[:, p2 * 2 * D:p2 * 2 * D + D],
                in_=y_d[q * 128:(q + 1) * 128, :]), w, f"ly{q}", stream="LA")
            w = [f"sx2_{q}"]
            if q >= 2:
                w.append(f"vo{q - 2}")
            ld_dma(lambda e, q=q, p2=p2: e.dma_start(
                out=xr_buf[:, p2 * D:(p2 + 1) * D],
                in_=x2_d[q * 128:(q + 1) * 128, :]), w, f"lx2{q}", stream="LX")
            wn = [f"ly{q}", "r4n"]
            if q >= 2:
                wn.append(f"vo{q - 2}")
            else:
                wn.append(f"sn3_{OWN - 2 + q}")
            s_op(lambda e, q=q, p2=p2: e.activation(
                evac[:, p2 * D:(p2 + 1) * D], x3bf[:, p2 * 2 * D:p2 * 2 * D + D],
                AF.Identity, bias=st4[:, 3 * OWN + q:3 * OWN + q + 1],
                scale=st4[:, 2 * OWN + q:2 * OWN + q + 1]), wn, f"n4_{q}")
            v_op(lambda e, q=q, p2=p2: e.tensor_tensor(
                x_in[:, p2 * D:(p2 + 1) * D], evac[:, p2 * D:(p2 + 1) * D],
                xr_buf[:, p2 * D:(p2 + 1) * D], ALU.add),
                (f"n4_{q}", f"lx2{q}"), f"vo{q}")
            st_dma(lambda e, q=q, p2=p2: e.dma_start(
                out=out_hbm[q * 128:(q + 1) * 128, :],
                in_=x_in[:, p2 * D:(p2 + 1) * D]), (f"vo{q}",), f"so{q}", stream="SO")

        if trunc is not None:
            del SCHED[int(len(SCHED) * trunc):]
        counts = {k: 0 for k in SEMS}
        EV = {}
        for eng, fn, waits, sig, semkey, inc, selfw in SCHED:
            if semkey != "?":
                counts[semkey] = counts.get(semkey, 0) + inc
                if sig is not None:
                    assert sig not in EV, f"duplicate event name {sig}"
                    EV[sig] = (semkey, counts[semkey])

        block = ctx.enter_context(nc.Block())

        def emit_for(eng_name, engine):
            hwm = {}

            def w(sk, val):
                if val <= 0 or hwm.get(sk, 0) >= val:
                    return
                hwm[sk] = val
                engine.wait_ge(SEMS[sk], val)

            for eng, fn, waits, sig, semkey, inc, selfw in SCHED:
                if eng != eng_name:
                    continue
                for wname in waits:
                    sk, val = EV[wname]
                    w(sk, val)
                if selfw is not None:
                    w(*selfw)
                ins = fn(engine)
                if semkey != "?":
                    ins.then_inc(SEMS[semkey], inc)
            if eng_name == "gpsimd":
                for sk, val in counts.items():
                    if sk not in ("S", "V", "T", "G") and val > 0:
                        w(sk, val)

        @block.sync
        def _(e):
            emit_for("sync", e)

        @block.scalar
        def _(e):
            emit_for("scalar", e)

        @block.vector
        def _(e):
            emit_for("vector", e)

        @block.tensor
        def _(e):
            emit_for("tensor", e)

        @block.gpsimd
        def _(e):
            emit_for("gpsimd", e)

    return nc


def make_in_maps(inputs, NT, D, FF):
    x = np.asarray(inputs["L0_attn_x"], np.float32)
    B = x.shape[0]
    planes = _host_planes()
    eyef = np.eye(128, dtype=np.float32)
    eyeb = np.eye(128).astype(ml_dtypes.bfloat16)
    wqk = np.ascontiguousarray(np.asarray(inputs["L0_attn_wqk"], np.float32))
    bqk = np.asarray(inputs["L0_attn_bqk"], np.float32)
    gamma = np.asarray(inputs["L0_attn_gamma"], np.float32)
    beta = np.asarray(inputs["L0_attn_beta"], np.float32)
    wv = np.ascontiguousarray(np.asarray(inputs["L0_attn_wv"], np.float32))
    bv = np.asarray(inputs["L0_attn_bv"], np.float32)
    relb = np.asarray(inputs["L0_attn_rel_bias"], np.float32)
    W1 = np.ascontiguousarray(np.asarray(inputs["L0_ffn_W1"], np.float32))
    b1 = np.asarray(inputs["L0_ffn_b1"], np.float32)
    W2 = np.ascontiguousarray(np.asarray(inputs["L0_ffn_W2"], np.float32))
    b2 = np.asarray(inputs["L0_ffn_b2"], np.float32)
    FC = FF // 128
    DCC = D // 128
    ppack = np.concatenate([bqk[None, :], gamma[0][None, :], beta[0][None, :],
                            gamma[1][None, :], beta[1][None, :],
                            b1.reshape(FC, 128), b2.reshape(DCC, 128)],
                           axis=0).astype(np.float32)
    in_maps, metas = [], []
    for c in range(8):
        g, k = (c // 4) % B, c % 4
        tiles, _ = _own_tiles(k, NT)
        xb = x[g]
        xown = np.concatenate([xb[t * 128:(t + 1) * 128] for t in tiles], axis=0)
        xcov = np.ascontiguousarray(np.concatenate([xb, xown], axis=0))
        in_maps.append({
            "xcov": xcov, "wqk": wqk, "wv": wv, "bv": bv, "relb": relb,
            "W1": W1, "W2": W2, "b2v": b2, "ppack": ppack, "planes": planes,
            "ohbnd": _host_ohbnd(k), "eyef": eyef,
            "eyeb": eyeb,
        })
        metas.append((g, tiles))
    return in_maps, metas


def kernel(**inputs):
    NT, D, FF = 64, 1024, 4096
    x = np.asarray(inputs["L0_attn_x"], np.float32)
    nc = build_nc(NT, D, FF)
    in_maps, metas = make_in_maps(inputs, NT, D, FF)
    res = run_bass_kernel_spmd(nc, in_maps, core_ids=list(range(8)))
    out = np.zeros_like(x)
    for c in range(8):
        g, tiles = metas[c]
        r = res.results[c]["out"]
        for i, t in enumerate(tiles):
            out[g, t * 128:(t + 1) * 128, :] = r[i * 128:(i + 1) * 128, :]
    return out
